# version 29
# baseline (speedup 1.0000x reference)
import numpy as np
import ml_dtypes
import concourse.bass as bass
import concourse.mybir as mybir
from concourse.bass_utils import run_bass_kernel_spmd
from contextlib import ExitStack
from itertools import chain

F32 = mybir.dt.float32
BF16 = mybir.dt.bfloat16
AF = mybir.ActivationFunctionType
OP = mybir.AluOpType

NH, DH, C, T = 16, 64, 1024, 2048
NCH = 8
NSEQ = 2
NBLK = 4
BT = 512
NLC = 16
SCALE = 0.125
EPS = 1e-5

QW = {"wq": 0, "wka": 4, "wvi": 8, "wo": 12, "wr": 16, "wfk": 20, "wfv": 24}
NQ = 28

ENGINES = ("tensor", "vector", "scalar", "gpsimd", "sync")


class Walk:

    def __init__(self, nc, en, eng_map, sems):
        self.nc = nc
        self.en = en
        self.eng = eng_map
        self.sems = sems
        self.cnt = {k: 0 for k in sems}
        self.hw = {}

    def wait(self, engine, sem, val):
        if val is None or val <= 0:
            return
        if self.en == engine:
            key = sem
            if self.hw.get(key, 0) >= val:
                return
            self.hw[key] = val
            self.eng[engine].wait_ge(self.sems[sem], val)

    OWN = {"vector": "v_dn", "scalar": "sc_dn"}

    def op(self, engine, fn, inc=None, inc_amt=1, lag=0):
        own = self.OWN.get(engine)
        if own is not None:
            self.wait(engine, own, self.cnt[own] - lag)
        if self.en == engine:
            inst = fn(self.eng[engine])
            if inc is not None:
                inst.then_inc(self.sems[inc], inc_amt)
        if inc is not None:
            self.cnt[inc] += inc_amt
        return self.cnt[inc] if inc is not None else None


def build_program():
    nc = bass.Bass("TRN2", target_bir_lowering=False)

    x_cm = nc.dram_tensor("x_cm", [NSEQ, C, T], F32, kind="ExternalInput")
    wpk = nc.dram_tensor("wpk", [NQ, 128, 2048], F32, kind="ExternalInput")
    bqkv = nc.dram_tensor("bqkv", [128, 24], F32, kind="ExternalInput")
    bo_in = nc.dram_tensor("bo_in", [128, 8], F32, kind="ExternalInput")
    mur_in = nc.dram_tensor("mur_in", [128, 8], F32, kind="ExternalInput")
    muk_in = nc.dram_tensor("muk_in", [128, 8], F32, kind="ExternalInput")
    ones_in = nc.dram_tensor("ones_in", [128, 128], BF16, kind="ExternalInput")
    ident_in = nc.dram_tensor("ident_in", [128, 128], F32, kind="ExternalInput")
    indic_in = nc.dram_tensor("indic_in", [16, 1024], BF16, kind="ExternalInput")
    out_cm = nc.dram_tensor("out_cm", [NSEQ, C, T], F32, kind="ExternalOutput")

    ctx = ExitStack()
    A = ctx.enter_context
    ar1 = A(nc.sbuf_tensor("ar1", [128, 16384], BF16))
    ln = A(nc.sbuf_tensor("ln", [128, 16384], BF16))
    qb = A(nc.sbuf_tensor("qb", [128, 16384], BF16))
    kb = A(nc.sbuf_tensor("kb", [128, 16384], BF16))
    ar6 = A(nc.sbuf_tensor("ar6", [128, 4096], BF16))
    x1 = A(nc.sbuf_tensor("x1", [128, 4096], F32))
    wbuf = A(nc.sbuf_tensor("wbuf", [128, 6144], BF16))
    opt = A(nc.sbuf_tensor("opt", [128, 4096], BF16))
    alphaT = A(nc.sbuf_tensor("alphaT", [16, 512], BF16))
    e_scr = A(nc.sbuf_tensor("e_scr", [128, 2048], BF16))
    xr_stage = A(nc.sbuf_tensor("xr_stage", [128, 1024], F32))
    out_stage = A(nc.sbuf_tensor("out_stage", [128, 1024], F32))
    v_scr = A(nc.sbuf_tensor("v_scr", [128, 1024], BF16))
    mean_s = A(nc.sbuf_tensor("mean_s", [128, 512], F32))
    var_s = A(nc.sbuf_tensor("var_s", [128, 512], F32))
    t3_s = A(nc.sbuf_tensor("t3_s", [128, 512], F32))
    t32_s = A(nc.sbuf_tensor("t32_s", [128, 512], F32))
    ab2_bf = A(nc.sbuf_tensor("ab2_bf", [128, 512], BF16))
    mb2_bf = A(nc.sbuf_tensor("mb2_bf", [128, 512], BF16))
    alpha_sb = A(nc.sbuf_tensor("alpha_sb", [128, 256], F32))
    z0_sb = A(nc.sbuf_tensor("z0_sb", [128, 256], F32))
    z1_sb = A(nc.sbuf_tensor("z1_sb", [128, 256], F32))
    ab_bf = A(nc.sbuf_tensor("ab_bf", [128, 512], BF16))
    mb_bf = A(nc.sbuf_tensor("mb_bf", [128, 512], BF16))
    kbar = A(nc.sbuf_tensor("kbar", [128, 48], F32))
    rt_scr = A(nc.sbuf_tensor("rt_scr", [128, 512], F32))
    prevc = A(nc.sbuf_tensor("prevc", [128, 8], BF16))
    eps_sb = A(nc.sbuf_tensor("eps_sb", [128, 1], F32))
    invT_sb = A(nc.sbuf_tensor("invT_sb", [128, 1], F32))
    ones_sb = A(nc.sbuf_tensor("ones_sb", [128, 128], BF16))
    ident_sb = A(nc.sbuf_tensor("ident_sb", [128, 128], F32))
    indic_sb = A(nc.sbuf_tensor("indic_sb", [16, 1024], BF16))
    bqkv_sb = A(nc.sbuf_tensor("bqkv_sb", [128, 24], F32))
    bo_sb = A(nc.sbuf_tensor("bo_sb", [128, 8], F32))
    mur_sb = A(nc.sbuf_tensor("mur_sb", [128, 8], F32))
    muk_sb = A(nc.sbuf_tensor("muk_sb", [128, 8], F32))

    pacc01 = A(nc.psum_tensor("pacc01", [128, 1024], F32))
    sg0 = A(nc.psum_tensor("sg0", [128, 1024], F32))
    sg1 = A(nc.psum_tensor("sg1", [128, 1024], F32))
    sg2 = A(nc.psum_tensor("sg2", [128, 1024], F32))

    def PA(i):
        return pacc01[:, (i % 2) * 512:((i % 2) + 1) * 512]
    sgs = [sg0, sg1, sg2]

    sem_names = ["g_x", "g_w0", "g_w1", "g_w2", "g_w3", "s_ld", "s_xr0", "s_xr1",
                 "s_st0", "s_st1", "t_mm", "v_dn", "sc_dn"]
    sems = {n: A(nc.semaphore(n)) for n in sem_names}
    block = A(nc.Block())

    qsched = []
    for s in range(NSEQ):
        for qi in range(4):
            qsched.append(QW["wq"] + qi)
            qsched.append(QW["wka"] + qi)
        for b in range(NBLK):
            for m in ("wvi", "wo", "wfk"):
                for qi in range(4):
                    qsched.append(QW[m] + qi)
            for qi in range(4):
                qsched.append(QW["wr"] + qi)
                qsched.append(QW["wfv"] + qi)
    NQS = len(qsched)

    def walk(en, engs):
        w = Walk(nc, en, engs, sems)
        slot_loads = [0, 0, 0]
        slot_free_t = [0, 0, 0]
        qpos = 0

        def load_quarter(i):
            nonlocal qpos
            assert i == qpos
            qpos += 1
            slot = i % 3
            src_q = qsched[i]
            w.wait("gpsimd", "t_mm", slot_free_t[slot])
            n = w.op("gpsimd",
                     lambda g, slot=slot, src_q=src_q: g.dma_start(
                         wbuf[:, slot * 2048:(slot + 1) * 2048], wpk[src_q, :, :]),
                     inc=f"g_w{slot}", inc_amt=16)
            slot_loads[slot] = n

        def use_quarter(i, last_mm_cnt):
            slot = i % 3
            slot_free_t[slot] = last_mm_cnt
            if qpos == i + 3 and qpos < NQS:
                load_quarter(i + 3)

        def wslot(i):
            return i % 3

        def wtile(i, dcl, cc):
            sl = wslot(i)
            return wbuf[:, sl * 2048 + dcl * 1024 + cc * 128:
                        sl * 2048 + dcl * 1024 + cc * 128 + 128]

        for fn in (lambda e: e.dma_start(ones_sb[:], ones_in[:, :]),
                   lambda e: e.dma_start(ident_sb[:], ident_in[:, :]),
                   lambda e: e.dma_start(indic_sb[:], indic_in[:, :]),
                   lambda e: e.dma_start(bqkv_sb[:], bqkv[:, :]),
                   lambda e: e.dma_start(bo_sb[:], bo_in[:, :]),
                   lambda e: e.dma_start(mur_sb[:], mur_in[:, :]),
                   lambda e: e.dma_start(muk_sb[:], muk_in[:, :])):
            w.op("sync", fn, inc="s_ld", inc_amt=16)
        n_const = w.cnt["s_ld"]

        for i in range(3):
            load_quarter(i)

        w.op("vector", lambda e: e.memset(eps_sb[:, :], float(EPS)), inc="v_dn")
        w.op("vector", lambda e: e.memset(invT_sb[:, :], 1.0 / T), inc="v_dn")

        pacc_evt = [[("v_dn", 0)], [("v_dn", 0)]]

        def wait_pacc(engine, i):
            for sem_, val_ in pacc_evt[i]:
                w.wait(engine, sem_, val_)

        def set_pacc(i, sem_, val_):
            pacc_evt[i] = [(sem_, val_)]
        x2slots_free = [0, 0]
        sg_free_sc = [0, 0, 0]
        sg_free_v = [0, 0, 0]
        zfree_v = [0, 0, 0, 0]
        sg_rr = 0
        acol_free_v = [0, 0, 0, 0]
        acol_free_t = [0, 0, 0, 0]
        rt_free_v = 0
        rpsum_free_sc = 0
        last_ar1_reader_v = 0
        last_ar1_reader_t = 0
        ln_free_v = 0
        ln_free_t = 0
        att_last_t = 0
        xr_slot_free = [[("v_dn", 0)], [("v_dn", 0)]]
        out_slot_free = [0, 0]
        xr_loads = [0, 0]
        x_cast_cnt = 0

        carry = None

        def xr_issue(s, b, do, slot):
            for sem_, val_ in xr_slot_free[slot]:
                w.wait("sync", sem_, val_)
            n = w.op("sync",
                     lambda e, s=s, b=b, do=do, slot=slot: e.dma_start(
                         xr_stage[:, slot * 512:(slot + 1) * 512],
                         x_cm[s, do * 128:(do + 1) * 128, b * 512:(b + 1) * 512]),
                     inc=f"s_xr{slot}", inc_amt=16)
            xr_loads[slot] = n

        ln_fill_v = [0] * 4

        def gen_s2a(si):
            nonlocal x_cast_cnt
            w.wait("gpsimd", "t_mm", att_last_t)
            x_cast_cnt = w.op(
                "gpsimd",
                lambda g, si=si: g.dma_start(
                    bass.AP(qb, 0, [[16384, 128], [2048, 8], [1, 2048]]),
                    bass.AP(x_cm, si * C * T, [[2048, 128], [128 * 2048, 8], [1, 2048]])),
                inc="g_x", inc_amt=16)
            yield 2
            for tt in range(4):
                sg = tt % 3
                w.wait("tensor", "g_x", x_cast_cnt)
                w.wait("tensor", "s_ld", n_const)
                w.wait("tensor", "sc_dn", sg_free_sc[sg])
                w.wait("tensor", "v_dn", sg_free_v[sg])
                mm_s1 = None
                for cc in range(NCH):
                    mm_s1 = w.op("tensor",
                                 lambda e, cc=cc, tt=tt, sg=sg: e.matmul(
                                     sgs[sg][:, 0:512],
                                     ones_sb[:, :],
                                     qb[:, cc * 2048 + tt * 512: cc * 2048 + (tt + 1) * 512],
                                     start=(cc == 0), stop=(cc == NCH - 1)),
                                 inc="t_mm" if cc == NCH - 1 else None)
                mm_s2 = None
                for cc in range(NCH):
                    xsl = cc % 2
                    w.wait("vector", "g_x", x_cast_cnt)
                    w.wait("vector", "sc_dn", max(sg_free_sc))
                    w.wait("vector", "t_mm", x2slots_free[xsl])
                    x2v = w.op("vector",
                               lambda e, cc=cc, tt=tt, xsl=xsl: e.tensor_tensor(
                                   e_scr[:, xsl * 512:(xsl + 1) * 512],
                                   qb[:, cc * 2048 + tt * 512: cc * 2048 + (tt + 1) * 512],
                                   qb[:, cc * 2048 + tt * 512: cc * 2048 + (tt + 1) * 512],
                                   op=OP.mult),
                               inc="v_dn", lag=1)
                    w.wait("tensor", "v_dn", x2v)
                    mm_s2 = w.op("tensor",
                                 lambda e, cc=cc, xsl=xsl, sg=sg: e.matmul(
                                     sgs[sg][:, 512:1024],
                                     ones_sb[:, :],
                                     e_scr[:, xsl * 512:(xsl + 1) * 512],
                                     start=(cc == 0), stop=(cc == NCH - 1)),
                                 inc="t_mm")
                    x2slots_free[xsl] = mm_s2
                yield 5
                w.wait("vector", "t_mm", mm_s1)
                pmv = w.op("vector", lambda e, sg=sg: e.tensor_copy(
                    xr_stage[:, 0:512], sgs[sg][:, 0:512]), inc="v_dn", lag=1)
                w.wait("vector", "t_mm", mm_s2)
                pav = w.op("vector", lambda e, sg=sg: e.tensor_copy(
                    xr_stage[:, 512:1024], sgs[sg][:, 512:1024]), inc="v_dn", lag=1)
                sg_free_v[sg] = pav
                w.op("vector", lambda e: e.tensor_tensor(
                    t32_s[:, :], xr_stage[:, 0:512], xr_stage[:, 0:512],
                    op=OP.mult), inc="v_dn", lag=1)
                vv = w.op("vector", lambda e: e.tensor_tensor(
                    xr_stage[:, 512:1024], xr_stage[:, 512:1024], t32_s[:, :],
                    op=OP.subtract), inc="v_dn")
                w.wait("scalar", "v_dn", vv)
                w.op("scalar", lambda e: e.activation(
                    xr_stage[:, 512:1024], xr_stage[:, 512:1024],
                    AF.Ln, bias=eps_sb[:, :]), inc="sc_dn")
                sc_a = w.op("scalar", lambda e: e.activation(
                    t32_s[:, :], xr_stage[:, 512:1024], AF.Exp, scale=-0.5), inc="sc_dn")
                w.wait("vector", "sc_dn", sc_a)
                w.op("vector", lambda e: e.tensor_tensor(
                    xr_stage[:, 0:512], xr_stage[:, 0:512], t32_s[:, :],
                    op=OP.mult), inc="v_dn")
                w.op("vector", lambda e: e.tensor_copy(ab2_bf[:, :], t32_s[:, :]), inc="v_dn", lag=1)
                mb2c = w.op("vector", lambda e: e.tensor_copy(
                    mb2_bf[:, :], xr_stage[:, 0:512]), inc="v_dn")
                w.wait("vector", "t_mm", x2slots_free[0])
                w.wait("vector", "t_mm", x2slots_free[1])
                lnv = None
                w.wait("vector", "t_mm", ln_free_t)
                for cc in range(NCH):
                    sl0 = (cc % 2) * 512
                    w.op("vector",
                         lambda e, cc=cc, tt=tt, sl0=sl0: e.tensor_tensor(
                             e_scr[:, sl0:sl0 + 512],
                             qb[:, cc * 2048 + tt * 512: cc * 2048 + (tt + 1) * 512],
                             ab2_bf[:, :], op=OP.mult),
                         inc="v_dn", lag=2)
                    lnv = w.op("vector",
                               lambda e, cc=cc, tt=tt, sl0=sl0: e.tensor_tensor(
                                   ln[:, cc * 2048 + tt * 512: cc * 2048 + (tt + 1) * 512],
                                   e_scr[:, sl0:sl0 + 512],
                                   mb2_bf[:, :], op=OP.subtract),
                               inc="v_dn")
                ln_fill_v[tt] = lnv
                x2slots_free[0] = 0
                yield 5
            xr_slot_free[0] = [("v_dn", mb2c)]
            xr_slot_free[1] = [("sc_dn", sc_a), ("v_dn", vv)]

        for s in range(NSEQ):
            sq = s * (8 + NBLK * 20)

            g2 = gen_s2a(s)
            if carry is not None:
                a3 = 0
                b3 = 0
                for c2 in g2:
                    a3 += c2
                    while carry is not None and b3 * 10 < a3 * 22:
                        try:
                            v = next(carry)
                        except StopIteration:
                            carry = None
                            break
                        if v >= 0:
                            b3 += v
                while carry is not None:
                    try:
                        next(carry)
                    except StopIteration:
                        carry = None
            else:
                for _ in g2:
                    pass

            kc_done_ch = [0] * NCH


            def emit_qkv_quarter(qi, which):
                gq = sq + qi * 2 + which
                slot = wslot(gq)
                is_k = which == 1
                dbase = qi * 2
                kevt = {}
                last_mm = 0
                for dcl in range(2):
                    d = dbase + dcl
                    for j in range(4):
                        tt = j
                        half = j % 2
                        w.wait("tensor", f"g_w{slot}", slot_loads[slot])
                        wait_pacc("tensor", half)
                        w.wait("tensor", "sc_dn", rpsum_free_sc)
                        w.wait("tensor", "v_dn", ln_fill_v[tt])
                        mmn = None
                        for cc in range(NCH):
                            mmn = w.op("tensor",
                                       lambda e, gq=gq, dcl=dcl, cc=cc, tt=tt, half=half: e.matmul(
                                           PA(half),
                                           wtile(gq, dcl, cc),
                                           ln[:, cc * 2048 + tt * 512: cc * 2048 + (tt + 1) * 512],
                                           start=(cc == 0), stop=(cc == NCH - 1)),
                                       inc="t_mm" if cc == NCH - 1 else None)
                        last_mm = mmn
                        use_sc = j % 2 == 0
                        dst = kb if is_k else qb
                        bcol = 8 + d if is_k else d
                        dsl = d * 2048 + j * 512
                        kslot = 4 * d + j
                        if use_sc:
                            w.wait("scalar", "t_mm", mmn)
                            if is_k:
                                ev = w.op("scalar",
                                          lambda e, dst=dst, dsl=dsl, bcol=bcol, kslot=kslot, half=half: e.activation(
                                              dst[:, dsl:dsl + 512],
                                              PA(half), AF.Identity,
                                              bias=bqkv_sb[:, bcol:bcol + 1],
                                              accum_out=kbar[:, kslot:kslot + 1]),
                                          inc="sc_dn", lag=1)
                            else:
                                ev = w.op("scalar",
                                          lambda e, dst=dst, dsl=dsl, bcol=bcol, half=half: e.activation(
                                              dst[:, dsl:dsl + 512],
                                              PA(half), AF.Identity,
                                              bias=bqkv_sb[:, bcol:bcol + 1]),
                                          inc="sc_dn", lag=1)
                            set_pacc(half, "sc_dn", ev)
                            kevt[j] = ("sc_dn", ev)
                        else:
                            w.wait("vector", "t_mm", mmn)
                            if is_k:
                                ev = w.op("vector",
                                          lambda e, dst=dst, dsl=dsl, bcol=bcol, kslot=kslot, half=half: e.tensor_scalar(
                                              dst[:, dsl:dsl + 512],
                                              PA(half), bqkv_sb[:, bcol:bcol + 1], 0.0,
                                              op0=OP.add, op1=OP.add,
                                              accum_out=kbar[:, kslot:kslot + 1]),
                                          inc="v_dn", lag=1)
                            else:
                                ev = w.op("vector",
                                          lambda e, dst=dst, dsl=dsl, bcol=bcol, half=half: e.tensor_scalar(
                                              dst[:, dsl:dsl + 512],
                                              PA(half), bqkv_sb[:, bcol:bcol + 1], 0.0,
                                              op0=OP.add, op1=OP.add),
                                          inc="v_dn", lag=1)
                            set_pacc(half, "v_dn", ev)
                            kevt[j] = ("v_dn", ev)
                    if is_k:
                        for j in range(4):
                            w.wait("vector", kevt[j][0], kevt[j][1])
                        w.op("vector", lambda e, d=d: e.tensor_tensor(
                            kbar[:, 32 + d:33 + d], kbar[:, 4 * d:4 * d + 1],
                            kbar[:, 4 * d + 1:4 * d + 2], op=OP.add), inc="v_dn")
                        w.op("vector", lambda e, d=d: e.tensor_tensor(
                            mean_s[:, 0:1], kbar[:, 4 * d + 2:4 * d + 3],
                            kbar[:, 4 * d + 3:4 * d + 4], op=OP.add), inc="v_dn")
                        w.op("vector", lambda e, d=d: e.tensor_tensor(
                            kbar[:, 32 + d:33 + d], kbar[:, 32 + d:33 + d],
                            mean_s[:, 0:1], op=OP.add), inc="v_dn")
                        w.op("vector", lambda e, d=d: e.tensor_scalar(
                            kbar[:, 32 + d:33 + d], kbar[:, 32 + d:33 + d],
                            invT_sb[:, 0:1], None, op0=OP.mult), inc="v_dn")
                        kc_done_ch[d] = w.op("vector",
                                             lambda e, d=d: e.tensor_scalar(
                                                 kb[:, d * 2048:(d + 1) * 2048],
                                                 kb[:, d * 2048:(d + 1) * 2048],
                                                 kbar[:, 32 + d:33 + d], None,
                                                 op0=OP.subtract),
                                             inc="v_dn")
                use_quarter(gq, last_mm)

            def emit_tile(p, lc, mh):
                nonlocal sg_rr
                bb = lc // 4
                gA = sg_rr % 3
                gB = (sg_rr + 1) % 3
                sg_rr += 2
                w.wait("tensor", "sc_dn", sg_free_sc[gA])
                w.wait("tensor", "sc_dn", sg_free_sc[gB])
                w.wait("tensor", "v_dn", sg_free_v[gA])
                w.wait("tensor", "v_dn", sg_free_v[gB])
                w.wait("tensor", "v_dn", kc_done_ch[p])
                mm_fill = [0, 0]
                for n in range(2):
                    for ab in range(2):
                        r0 = ab * 64
                        g = gA if ab == 0 else gB
                        mm_fill[ab] = w.op(
                            "tensor",
                            lambda e, p=p, lc=lc, mh=mh, n=n, r0=r0, g=g: e.matmul(
                                sgs[g][:, n * 512:(n + 1) * 512],
                                qb[r0:r0 + 64,
                                   p * 2048 + lc * 128: p * 2048 + (lc + 1) * 128],
                                kb[r0:r0 + 64,
                                   p * 2048 + mh * 1024 + n * 512:
                                   p * 2048 + mh * 1024 + (n + 1) * 512],
                                start=True, stop=True,
                                tile_position=(r0, 0)),
                            inc="t_mm" if n == 1 else None)
                for ab in range(2):
                    g = gA if ab == 0 else gB
                    zt = z0_sb if mh == 0 else z1_sb
                    col = lc * 16 + 2 * p + ab
                    w.wait("scalar", "t_mm", mm_fill[ab])
                    w.wait("scalar", "v_dn", zfree_v[bb])
                    sg_free_sc[g] = w.op(
                        "scalar",
                        lambda e, ab=ab, col=col, g=g, zt=zt: e.activation(
                            e_scr[:, ab * 1024:(ab + 1) * 1024],
                            sgs[g][:, :], AF.Exp,
                            scale=float(SCALE),
                            accum_out=zt[:, col:col + 1]),
                        inc="sc_dn", lag=1)

            for qi in range(4):
                emit_qkv_quarter(qi, 0)
                emit_qkv_quarter(qi, 1)
                for p in (2 * qi, 2 * qi + 1):
                    for lci in range(4):
                        for mh in range(2):
                            emit_tile(p, 0 * 4 + lci, mh)

            kc_done = max(kc_done_ch)

            def gen_attn(b, kc_done):
                for lci in range(4):
                    lc = b * 4 + lci
                    for p in range(NCH):
                        for mh in range(2):
                            emit_tile(p, lc, mh)
                            yield 4

            def gen_post(b, qbase, si):
                nonlocal last_ar1_reader_v, last_ar1_reader_t, rt_free_v, rpsum_free_sc
                w.wait("vector", "sc_dn", max(sg_free_sc))
                w.wait("vector", "v_dn", acol_free_v[b])
                w.wait("vector", "t_mm", acol_free_t[b])
                zfree_v[b] = w.op("vector", lambda e, b=b: e.tensor_tensor(
                    alpha_sb[:, b * 64:(b + 1) * 64],
                    z0_sb[:, b * 64:(b + 1) * 64],
                    z1_sb[:, b * 64:(b + 1) * 64], op=OP.add), inc="v_dn")
                w.op("vector", lambda e, b=b: e.reciprocal(
                    var_s[:, 0:64], alpha_sb[:, b * 64:(b + 1) * 64]), inc="v_dn")
                amul = w.op("vector", lambda e, b=b: e.tensor_tensor(
                    alpha_sb[:, b * 64:(b + 1) * 64],
                    alpha_sb[:, b * 64:(b + 1) * 64],
                    var_s[:, 0:64], op=OP.mult), inc="v_dn")
                acol_free_v[b] = amul
                yield 6
                w.wait("tensor", "v_dn", amul)
                wait_pacc("tensor", 0)
                tcnt = 0
                for lci in range(4):
                    lc = b * 4 + lci
                    tcnt = w.op("tensor",
                                lambda e, lc=lc, lci=lci: e.transpose(
                                    pacc01[0:16, lci * 128:(lci + 1) * 128],
                                    alpha_sb[:, lc * 16:(lc + 1) * 16],
                                    ident_sb[:, :]),
                                inc="t_mm" if lci == 3 else None)
                acol_free_t[b] = tcnt
                w.wait("vector", "t_mm", tcnt)
                atv = w.op("vector",
                           lambda e: e.tensor_copy(
                               alphaT[0:16, :], pacc01[0:16, 0:512]),
                           inc="v_dn", lag=1)
                set_pacc(0, "v_dn", atv)
                alphaT_ready_v = atv
                yield 6

                for do in range(2):
                    xr_issue(si, b, do, do % 2)
                vop = 0
                for dv in range(NCH):
                    gq = qbase + (dv // 2)
                    slot = wslot(gq)
                    dcl = dv % 2
                    pa = dv % 2
                    pb = 1 - pa
                    w.wait("tensor", f"g_w{slot}", slot_loads[slot])
                    wait_pacc("tensor", pa)
                    mmv = None
                    for cc in range(NCH):
                        mmv = w.op("tensor",
                                   lambda e, gq=gq, dcl=dcl, cc=cc, b=b, pa=pa: e.matmul(
                                       PA(pa),
                                       wtile(gq, dcl, cc),
                                       ln[:, cc * 2048 + b * 512: cc * 2048 + (b + 1) * 512],
                                       start=(cc == 0), stop=(cc == NCH - 1)),
                                   inc="t_mm" if cc == NCH - 1 else None)
                    if dv % 2 == 1:
                        use_quarter(gq, mmv)
                    w.wait("tensor", "v_dn", alphaT_ready_v)
                    wait_pacc("tensor", pb)
                    abmm = w.op("tensor",
                                lambda e, dv=dv, pb=pb: e.matmul(
                                    PA(pb),
                                    indic_sb[:, dv * 128:(dv + 1) * 128],
                                    alphaT[0:16, :],
                                    start=True, stop=True),
                                inc="t_mm")
                    vsl = (dv % 2) * 512
                    w.wait("vector", "t_mm", mmv)
                    vev = w.op("vector",
                               lambda e, dv=dv, pa=pa, vsl=vsl: e.tensor_scalar(
                                   v_scr[:, vsl:vsl + 512], PA(pa),
                                   bqkv_sb[:, 16 + dv:17 + dv], 0.0,
                                   op0=OP.add, op1=OP.add),
                               inc="v_dn", lag=1)
                    set_pacc(pa, "v_dn", vev)
                    w.wait("vector", "t_mm", abmm)
                    vop = w.op("vector",
                               lambda e, dv=dv, vsl=vsl, pb=pb: e.tensor_tensor(
                                   opt[:, dv * 512:(dv + 1) * 512],
                                   v_scr[:, vsl:vsl + 512], PA(pb), op=OP.mult),
                               inc="v_dn")
                    set_pacc(pb, "v_dn", vop)
                    yield 9
                opt_fill_v = vop
                yield -1000 - b

                x1v = 0
                for do in range(NCH):
                    gq = qbase + 4 + (do // 2)
                    slot = wslot(gq)
                    dcl = do % 2
                    pa = do % 2
                    w.wait("tensor", f"g_w{slot}", slot_loads[slot])
                    wait_pacc("tensor", pa)
                    w.wait("tensor", "v_dn", opt_fill_v)
                    mmo = None
                    for dv in range(NCH):
                        mmo = w.op("tensor",
                                   lambda e, gq=gq, dcl=dcl, dv=dv, pa=pa: e.matmul(
                                       PA(pa),
                                       wtile(gq, dcl, dv),
                                       opt[:, dv * 512:(dv + 1) * 512],
                                       start=(dv == 0), stop=(dv == NCH - 1)),
                                   inc="t_mm" if dv == NCH - 1 else None)
                    if do % 2 == 1:
                        use_quarter(gq, mmo)
                    xsl = do % 2
                    w.wait("vector", "t_mm", mmo)
                    w.wait("vector", f"s_xr{xsl}", xr_loads[xsl])
                    x1v = w.op("vector",
                               lambda e, do=do, pa=pa, xsl=xsl: e.scalar_tensor_tensor(
                                   x1[:, do * 512:(do + 1) * 512],
                                   PA(pa), bo_sb[:, do:do + 1],
                                   xr_stage[:, xsl * 512:(xsl + 1) * 512],
                                   op0=OP.add, op1=OP.add),
                               inc="v_dn", lag=2)
                    set_pacc(pa, "v_dn", x1v)
                    xr_slot_free[xsl] = [("v_dn", x1v)]
                    if do + 2 < NCH:
                        xr_issue(si, b, do + 2, (do + 2) % 2)
                    yield 8

                yield -1 - b
                xbv = w.op("vector", lambda e: e.tensor_copy(
                    ar1[:, 0:4096], x1[:, :]), inc="v_dn")
                last_ar1_reader_v = xbv
                w.wait("tensor", "v_dn", xbv)
                wait_pacc("tensor", 0)
                mm_s1 = None
                for cc in range(NCH):
                    mm_s1 = w.op("tensor",
                                 lambda e, cc=cc: e.matmul(
                                     PA(0), ones_sb[:, :],
                                     ar1[:, cc * 512:(cc + 1) * 512],
                                     start=(cc == 0), stop=(cc == NCH - 1)),
                                 inc="t_mm" if cc == NCH - 1 else None)
                mm_s2 = None
                for cc in range(NCH):
                    xsl = cc % 2
                    w.wait("vector", "t_mm", x2slots_free[xsl])
                    x2v = w.op("vector",
                               lambda e, cc=cc, xsl=xsl: e.tensor_tensor(
                                   ar1[:, 8192 + xsl * 512: 8192 + (xsl + 1) * 512],
                                   ar1[:, cc * 512:(cc + 1) * 512],
                                   ar1[:, cc * 512:(cc + 1) * 512],
                                   op=OP.mult),
                               inc="v_dn", lag=(0 if cc == 0 else 1))
                    w.wait("tensor", "v_dn", x2v)
                    if cc == 0:
                        wait_pacc("tensor", 1)
                    mm_s2 = w.op("tensor",
                                 lambda e, cc=cc, xsl=xsl: e.matmul(
                                     PA(1), ones_sb[:, :],
                                     ar1[:, 8192 + xsl * 512: 8192 + (xsl + 1) * 512],
                                     start=(cc == 0), stop=(cc == NCH - 1)),
                                 inc="t_mm")
                    x2slots_free[xsl] = mm_s2
                yield 16
                w.wait("vector", "t_mm", mm_s1)
                set_pacc(0, "v_dn", w.op("vector", lambda e: e.tensor_copy(
                    mean_s[:, :], PA(0)), inc="v_dn", lag=1))
                w.wait("vector", "t_mm", mm_s2)
                set_pacc(1, "v_dn", w.op("vector", lambda e: e.tensor_copy(
                    var_s[:, :], PA(1)), inc="v_dn", lag=1))
                w.op("vector", lambda e: e.tensor_tensor(
                    t3_s[:, :], mean_s[:, :], mean_s[:, :], op=OP.mult), inc="v_dn", lag=1)
                vv = w.op("vector", lambda e: e.tensor_tensor(
                    var_s[:, :], var_s[:, :], t3_s[:, :], op=OP.subtract), inc="v_dn")
                w.wait("scalar", "v_dn", vv)
                w.op("scalar", lambda e: e.activation(
                    var_s[:, :], var_s[:, :], AF.Ln, bias=eps_sb[:, :]), inc="sc_dn")
                sc_a = w.op("scalar", lambda e: e.activation(
                    t3_s[:, :], var_s[:, :], AF.Exp, scale=-0.5), inc="sc_dn")
                w.wait("vector", "sc_dn", sc_a)
                w.op("vector", lambda e: e.tensor_tensor(
                    mean_s[:, :], mean_s[:, :], t3_s[:, :], op=OP.mult), inc="v_dn")
                w.op("vector", lambda e: e.tensor_copy(ab_bf[:, :], t3_s[:, :]), inc="v_dn", lag=1)
                w.op("vector", lambda e: e.tensor_copy(mb_bf[:, :], mean_s[:, :]), inc="v_dn")
                yield 4
                w.wait("vector", "t_mm", max(x2slots_free))
                for cc in range(NCH):
                    w.op("vector",
                         lambda e, cc=cc: e.tensor_tensor(
                             ar1[:, 8192 + cc * 512: 8192 + (cc + 1) * 512],
                             ar1[:, cc * 512:(cc + 1) * 512],
                             ab_bf[:, :], op=OP.mult),
                         inc="v_dn", lag=2)
                    w.op("vector",
                         lambda e, cc=cc: e.tensor_tensor(
                             ar1[:, 4096 + cc * 512: 4096 + (cc + 1) * 512],
                             ar1[:, 8192 + cc * 512: 8192 + (cc + 1) * 512],
                             mb_bf[:, :], op=OP.subtract),
                         inc="v_dn")
                    if cc % 2 == 1:
                        yield 3
                if b == 0:
                    w.op("vector", lambda e: e.memset(prevc[:, :], 0.0), inc="v_dn")
                yield 4
                for cc in range(NCH):
                    c0 = 4096 + cc * 512
                    dd0 = 8192 + cc * 512
                    w.op("vector",
                         lambda e, c0=c0, dd0=dd0: e.tensor_tensor(
                             bass.AP(ar1, dd0 + 1, [[16384, 128], [1, 511]]),
                             bass.AP(ar1, c0 + 1, [[16384, 128], [1, 511]]),
                             bass.AP(ar1, c0, [[16384, 128], [1, 511]]),
                             op=OP.subtract),
                         inc="v_dn", lag=2)
                w.op("vector",
                     lambda e: e.tensor_tensor(
                         bass.AP(ar1, 8192, [[16384, 128], [512, 8]]),
                         bass.AP(ar1, 4096, [[16384, 128], [512, 8]]),
                         prevc[:, 0:8], op=OP.subtract),
                     inc="v_dn", lag=1)
                yield 6
                for cc in range(NCH):
                    c0 = 4096 + cc * 512
                    dd0 = 8192 + cc * 512
                    w.op("vector",
                         lambda e, c0=c0, dd0=dd0, cc=cc: e.scalar_tensor_tensor(
                             bass.AP(ar1, 12288 + cc * 512 + 1, [[16384, 128], [1, 511]]),
                             bass.AP(ar1, dd0 + 1, [[16384, 128], [1, 511]]),
                             mur_sb[:, cc:cc + 1],
                             bass.AP(ar1, c0, [[16384, 128], [1, 511]]),
                             op0=OP.mult, op1=OP.add),
                         inc="v_dn", lag=2)
                    w.op("vector",
                         lambda e, dd0=dd0, cc=cc: e.scalar_tensor_tensor(
                             ar1[:, 12288 + cc * 512: 12288 + cc * 512 + 1],
                             ar1[:, dd0:dd0 + 1], mur_sb[:, cc:cc + 1],
                             prevc[:, cc:cc + 1], op0=OP.mult, op1=OP.add),
                         inc="v_dn")
                    w.op("vector",
                         lambda e, c0=c0, dd0=dd0, cc=cc: e.scalar_tensor_tensor(
                             bass.AP(ar6, cc * 512 + 1, [[4096, 128], [1, 511]]),
                             bass.AP(ar1, dd0 + 1, [[16384, 128], [1, 511]]),
                             muk_sb[:, cc:cc + 1],
                             bass.AP(ar1, c0, [[16384, 128], [1, 511]]),
                             op0=OP.mult, op1=OP.add),
                         inc="v_dn", lag=2)
                    w.op("vector",
                         lambda e, dd0=dd0, cc=cc: e.scalar_tensor_tensor(
                             ar6[:, cc * 512: cc * 512 + 1],
                             ar1[:, dd0:dd0 + 1], muk_sb[:, cc:cc + 1],
                             prevc[:, cc:cc + 1], op0=OP.mult, op1=OP.add),
                         inc="v_dn")
                    if cc % 2 == 1:
                        yield 4
                mixv = w.op("vector", lambda e: e.tensor_copy(
                    prevc[:, :],
                    bass.AP(ar1, 4096 + 511, [[16384, 128], [512, 8]])), inc="v_dn")
                last_ar1_reader_v = mixv
                yield 8

                kav = 0
                for dd in range(NCH):
                    gq = qbase + 8 + (dd // 2)
                    slot = wslot(gq)
                    dcl = dd % 2
                    pa = dd % 2
                    w.wait("tensor", f"g_w{slot}", slot_loads[slot])
                    w.wait("tensor", "v_dn", mixv)
                    wait_pacc("tensor", pa)
                    mmk = None
                    for cc in range(NCH):
                        mmk = w.op("tensor",
                                   lambda e, gq=gq, dcl=dcl, cc=cc, pa=pa: e.matmul(
                                       PA(pa),
                                       wtile(gq, dcl, cc),
                                       ar6[:, cc * 512:(cc + 1) * 512],
                                       start=(cc == 0), stop=(cc == NCH - 1)),
                                   inc="t_mm" if cc == NCH - 1 else None)
                    if dd % 2 == 1:
                        use_quarter(gq, mmk)
                    w.wait("vector", "t_mm", mmk)
                    kasc = var_s if dd % 2 == 0 else mean_s
                    w.op("vector",
                         lambda e, pa=pa, kasc=kasc: e.tensor_scalar(
                             kasc[:, :], PA(pa), 0.0, None, op0=OP.max),
                         inc="v_dn", lag=2)
                    kav = w.op("vector",
                               lambda e, dd=dd, kasc=kasc: e.tensor_tensor(
                                   ar1[:, 8192 + dd * 512: 8192 + (dd + 1) * 512],
                                   kasc[:, :], kasc[:, :], op=OP.mult),
                               inc="v_dn")
                    set_pacc(pa, "v_dn", kav)
                    yield 8
                last_ar1_reader_v = kav

                mmf = 0
                for dd in range(NCH):
                    gqr = qbase + 12 + 2 * (dd // 2)
                    gqv = gqr + 1
                    dcl = dd % 2
                    sr = wslot(gqr)
                    sv = wslot(gqv)
                    pfr = dd % 2
                    pff = 1 - pfr
                    w.wait("tensor", f"g_w{sr}", slot_loads[sr])
                    w.wait("tensor", "v_dn", kav)
                    wait_pacc("tensor", pfr)
                    w.wait("tensor", "sc_dn", rpsum_free_sc)
                    mmr = None
                    for cc in range(NCH):
                        mmr = w.op("tensor",
                                   lambda e, gqr=gqr, dcl=dcl, cc=cc, pfr=pfr: e.matmul(
                                       PA(pfr),
                                       wtile(gqr, dcl, cc),
                                       ar1[:, 12288 + cc * 512: 12288 + (cc + 1) * 512],
                                       start=(cc == 0), stop=(cc == NCH - 1)),
                                   inc="t_mm" if cc == NCH - 1 else None)
                    yield 8
                    w.wait("tensor", f"g_w{sv}", slot_loads[sv])
                    wait_pacc("tensor", pff)
                    mmf = None
                    for cc in range(NCH):
                        mmf = w.op("tensor",
                                   lambda e, gqv=gqv, dcl=dcl, cc=cc, pff=pff: e.matmul(
                                       PA(pff),
                                       wtile(gqv, dcl, cc),
                                       ar1[:, 8192 + cc * 512: 8192 + (cc + 1) * 512],
                                       start=(cc == 0), stop=(cc == NCH - 1)),
                                   inc="t_mm" if cc == NCH - 1 else None)
                    if dd % 2 == 1:
                        use_quarter(gqr, mmf)
                        use_quarter(gqv, mmf)
                    w.wait("scalar", "t_mm", mmr)
                    w.wait("scalar", "v_dn", rt_free_v)
                    rpsum_free_sc = w.op("scalar",
                         lambda e, pfr=pfr: e.activation(
                             rt_scr[:, :], PA(pfr), AF.Exp, scale=-1.0),
                         inc="sc_dn", lag=1)
                    w.op("scalar", lambda e: e.activation(
                        rt_scr[:, :], rt_scr[:, :], AF.Ln, bias=1.0), inc="sc_dn")
                    scs = w.op("scalar", lambda e: e.activation(
                        rt_scr[:, :], rt_scr[:, :], AF.Exp, scale=-1.0), inc="sc_dn")
                    osl = dd % 2
                    w.wait("vector", "sc_dn", scs)
                    w.wait("vector", "t_mm", mmf)
                    w.wait("vector", f"s_st{osl}", out_slot_free[osl])
                    fsc = t3_s if dd % 2 == 0 else mean_s
                    t1v = w.op("vector",
                               lambda e, pff=pff, fsc=fsc: e.tensor_tensor(
                                   fsc[:, :], PA(pff), rt_scr[:, :], op=OP.mult),
                               inc="v_dn", lag=1)
                    rt_free_v = t1v
                    set_pacc(pff, "v_dn", t1v)
                    ov = w.op("vector",
                              lambda e, dd=dd, osl=osl, fsc=fsc: e.tensor_tensor(
                                  out_stage[:, osl * 512:(osl + 1) * 512],
                                  fsc[:, :], x1[:, dd * 512:(dd + 1) * 512],
                                  op=OP.add),
                              inc="v_dn")
                    w.wait("sync", "v_dn", ov)
                    stn = w.op("sync",
                               lambda e, si=si, b=b, dd=dd, osl=osl: e.dma_start(
                                   out_cm[si, dd * 128:(dd + 1) * 128,
                                          b * 512:(b + 1) * 512],
                                   out_stage[:, osl * 512:(osl + 1) * 512]),
                               inc=f"s_st{osl}", inc_amt=16)
                    out_slot_free[osl] = stn
                    yield 8
                last_ar1_reader_t = w.cnt["t_mm"]

            for b in range(NBLK):
                a_cost = 0
                b_cost = 0
                if b > 0:
                    ga = gen_attn(b, kc_done)
                    for c in ga:
                        a_cost += c
                        while carry is not None and b_cost * 20 < max(0, a_cost - 8) * 27:
                            try:
                                v = next(carry)
                                if v >= 0:
                                    b_cost += v
                            except StopIteration:
                                carry = None
                if b == NBLK - 1:
                    att_last_t = w.cnt["t_mm"]
                ng = gen_post(b, sq + 8 + b * 20, s)
                carry = ng if carry is None else chain(carry, ng)
            if s == NSEQ - 1:
                for _ in carry:
                    pass
                carry = None
            else:
                for c in carry:
                    if c == -1 - (NBLK - 1):
                        break
                ln_free_t = w.cnt["t_mm"]

        w.wait("sync", "s_st0", w.cnt["s_st0"])
        w.wait("sync", "s_st1", w.cnt["s_st1"])
        return w

    @block.tensor
    def _(e):
        walk("tensor", {"tensor": e})

    @block.vector
    def _(e):
        walk("vector", {"vector": e})

    @block.scalar
    def _(e):
        walk("scalar", {"scalar": e})

    @block.gpsimd
    def _(e):
        walk("gpsimd", {"gpsimd": e})

    @block.sync
    def _(e):
        walk("sync", {"sync": e})

    ctx.close()
    return nc


def _pack_quarters(MT):
    t = MT.reshape(8, 128, 8, 128)
    out = np.empty((4, 128, 2048), np.float32)
    for qi in range(4):
        blk = t[:, :, 2 * qi:2 * qi + 2, :]
        out[qi] = blk.transpose(1, 2, 0, 3).reshape(128, 2048)
    return out


def _host_prep(inputs):
    w_in = np.asarray(inputs["L0_attn_w_in"], np.float32)
    b_in = np.asarray(inputs["L0_attn_b_in"], np.float32)
    perm_q = [h * 192 + i for h in range(NH) for i in range(64)]
    perm_k = [h * 192 + 64 + i for h in range(NH) for i in range(64)]
    perm_v = [h * 192 + 128 + i for h in range(NH) for i in range(64)]
    wqT = np.ascontiguousarray(w_in[perm_q].T)
    wkT = np.ascontiguousarray(w_in[perm_k].T)
    wvT = np.ascontiguousarray(w_in[perm_v].T)
    woT = np.ascontiguousarray(np.asarray(inputs["L0_attn_w_out"], np.float32).T)
    wrT = np.ascontiguousarray(np.asarray(inputs["L0_ffn_W_r"], np.float32).T)
    wfkT = np.ascontiguousarray(np.asarray(inputs["L0_ffn_W_k"], np.float32).T)
    wfvT = np.ascontiguousarray(np.asarray(inputs["L0_ffn_W_v"], np.float32).T)

    wpk = np.empty((NQ, 128, 2048), np.float32)
    wpk[QW["wq"]:QW["wq"] + 4] = _pack_quarters(wqT)
    wpk[QW["wka"]:QW["wka"] + 4] = _pack_quarters(wkT)
    wpk[QW["wvi"]:QW["wvi"] + 4] = _pack_quarters(wvT)
    wpk[QW["wo"]:QW["wo"] + 4] = _pack_quarters(woT)
    wpk[QW["wr"]:QW["wr"] + 4] = _pack_quarters(wrT)
    wpk[QW["wfk"]:QW["wfk"] + 4] = _pack_quarters(wfkT)
    wpk[QW["wfv"]:QW["wfv"] + 4] = _pack_quarters(wfvT)

    bqkv = np.empty((128, 24), np.float32)
    bqkv[:, 0:8] = b_in[perm_q].reshape(8, 128).T
    bqkv[:, 8:16] = b_in[perm_k].reshape(8, 128).T
    bqkv[:, 16:24] = b_in[perm_v].reshape(8, 128).T
    bo = np.ascontiguousarray(
        np.asarray(inputs["L0_attn_b_out"], np.float32).reshape(8, 128).T)
    mur = np.ascontiguousarray(
        np.asarray(inputs["L0_ffn_mu_r"], np.float32).reshape(8, 128).T)
    muk = np.ascontiguousarray(
        np.asarray(inputs["L0_ffn_mu_k"], np.float32).reshape(8, 128).T)

    ones128 = np.full((128, 128), 1.0 / C, ml_dtypes.bfloat16)
    ident = np.eye(128, dtype=np.float32)
    indic = np.zeros((16, 1024), ml_dtypes.bfloat16)
    for dv in range(8):
        for p in range(128):
            indic[2 * dv + p // 64, dv * 128 + p] = 1.0
    return dict(wpk=wpk, bqkv=bqkv, bo_in=bo, mur_in=mur, muk_in=muk,
                ones_in=ones128, ident_in=ident, indic_in=indic)


def kernel(**inputs):
    x = np.asarray(inputs["L0_attn_x"], np.float32)
    shared = _host_prep(inputs)
    nc = build_program()
    in_maps = []
    for core in range(8):
        xs = x[core * NSEQ:(core + 1) * NSEQ]
        x_cmv = np.ascontiguousarray(xs.transpose(0, 2, 1))
        m = dict(shared)
        m["x_cm"] = x_cmv
        in_maps.append(m)
    res = run_bass_kernel_spmd(nc, in_maps, core_ids=list(range(8)))
    outs = [r["out_cm"] for r in res.results]
    out = np.concatenate(outs, axis=0).transpose(0, 2, 1)
    return np.ascontiguousarray(out).astype(np.float32)


if __name__ == "__main__":
    nc = build_program()
    print("built OK")


# version 30
# speedup vs baseline: 1.0076x; 1.0076x over previous
import numpy as np
import ml_dtypes
import concourse.bass as bass
import concourse.mybir as mybir
from concourse.bass_utils import run_bass_kernel_spmd
from contextlib import ExitStack
from itertools import chain

F32 = mybir.dt.float32
BF16 = mybir.dt.bfloat16
AF = mybir.ActivationFunctionType
OP = mybir.AluOpType

NH, DH, C, T = 16, 64, 1024, 2048
NCH = 8
NSEQ = 2
NBLK = 4
BT = 512
NLC = 16
SCALE = 0.125
EPS = 1e-5

QW = {"wq": 0, "wka": 4, "wvi": 8, "wo": 12, "wr": 16, "wfk": 20, "wfv": 24}
NQ = 28

ENGINES = ("tensor", "vector", "scalar", "gpsimd", "sync")


class Walk:

    def __init__(self, nc, en, eng_map, sems):
        self.nc = nc
        self.en = en
        self.eng = eng_map
        self.sems = sems
        self.cnt = {k: 0 for k in sems}
        self.hw = {}

    def wait(self, engine, sem, val):
        if val is None or val <= 0:
            return
        if self.en == engine:
            key = sem
            if self.hw.get(key, 0) >= val:
                return
            self.hw[key] = val
            self.eng[engine].wait_ge(self.sems[sem], val)

    OWN = {"vector": "v_dn", "scalar": "sc_dn"}

    def op(self, engine, fn, inc=None, inc_amt=1, lag=0):
        own = self.OWN.get(engine)
        if own is not None:
            self.wait(engine, own, self.cnt[own] - lag)
        if self.en == engine:
            inst = fn(self.eng[engine])
            if inc is not None:
                inst.then_inc(self.sems[inc], inc_amt)
        if inc is not None:
            self.cnt[inc] += inc_amt
        return self.cnt[inc] if inc is not None else None


def build_program():
    nc = bass.Bass("TRN2", target_bir_lowering=False)

    x_cm = nc.dram_tensor("x_cm", [NSEQ, C, T], F32, kind="ExternalInput")
    wpk = nc.dram_tensor("wpk", [NQ, 128, 2048], F32, kind="ExternalInput")
    bqkv = nc.dram_tensor("bqkv", [128, 24], F32, kind="ExternalInput")
    bo_in = nc.dram_tensor("bo_in", [128, 8], F32, kind="ExternalInput")
    mur_in = nc.dram_tensor("mur_in", [128, 8], F32, kind="ExternalInput")
    muk_in = nc.dram_tensor("muk_in", [128, 8], F32, kind="ExternalInput")
    ones_in = nc.dram_tensor("ones_in", [128, 128], BF16, kind="ExternalInput")
    ident_in = nc.dram_tensor("ident_in", [128, 128], F32, kind="ExternalInput")
    indic_in = nc.dram_tensor("indic_in", [16, 1024], BF16, kind="ExternalInput")
    out_cm = nc.dram_tensor("out_cm", [NSEQ, C, T], F32, kind="ExternalOutput")

    ctx = ExitStack()
    A = ctx.enter_context
    ar1 = A(nc.sbuf_tensor("ar1", [128, 16384], BF16))
    ln = A(nc.sbuf_tensor("ln", [128, 16384], BF16))
    qb = A(nc.sbuf_tensor("qb", [128, 16384], BF16))
    kb = A(nc.sbuf_tensor("kb", [128, 16384], BF16))
    ar6 = A(nc.sbuf_tensor("ar6", [128, 4096], BF16))
    x1 = A(nc.sbuf_tensor("x1", [128, 4096], F32))
    wbuf = A(nc.sbuf_tensor("wbuf", [128, 6144], BF16))
    opt = A(nc.sbuf_tensor("opt", [128, 4096], BF16))
    alphaT = A(nc.sbuf_tensor("alphaT", [16, 512], BF16))
    e_scr = A(nc.sbuf_tensor("e_scr", [128, 2048], BF16))
    xr_stage = A(nc.sbuf_tensor("xr_stage", [128, 1024], F32))
    out_stage = A(nc.sbuf_tensor("out_stage", [128, 1024], F32))
    v_scr = A(nc.sbuf_tensor("v_scr", [128, 1024], BF16))
    mean_s = A(nc.sbuf_tensor("mean_s", [128, 512], F32))
    var_s = A(nc.sbuf_tensor("var_s", [128, 512], F32))
    t3_s = A(nc.sbuf_tensor("t3_s", [128, 512], F32))
    t32_s = A(nc.sbuf_tensor("t32_s", [128, 512], F32))
    ab2_bf = A(nc.sbuf_tensor("ab2_bf", [128, 512], BF16))
    mb2_bf = A(nc.sbuf_tensor("mb2_bf", [128, 512], BF16))
    alpha_sb = A(nc.sbuf_tensor("alpha_sb", [128, 256], F32))
    z0_sb = A(nc.sbuf_tensor("z0_sb", [128, 256], F32))
    z1_sb = A(nc.sbuf_tensor("z1_sb", [128, 256], F32))
    ab_bf = A(nc.sbuf_tensor("ab_bf", [128, 512], BF16))
    mb_bf = A(nc.sbuf_tensor("mb_bf", [128, 512], BF16))
    kbar = A(nc.sbuf_tensor("kbar", [128, 48], F32))
    rt_scr = A(nc.sbuf_tensor("rt_scr", [128, 512], F32))
    prevc = A(nc.sbuf_tensor("prevc", [128, 8], BF16))
    eps_sb = A(nc.sbuf_tensor("eps_sb", [128, 1], F32))
    invT_sb = A(nc.sbuf_tensor("invT_sb", [128, 1], F32))
    ones_sb = A(nc.sbuf_tensor("ones_sb", [128, 128], BF16))
    ident_sb = A(nc.sbuf_tensor("ident_sb", [128, 128], F32))
    indic_sb = A(nc.sbuf_tensor("indic_sb", [16, 1024], BF16))
    bqkv_sb = A(nc.sbuf_tensor("bqkv_sb", [128, 24], F32))
    bo_sb = A(nc.sbuf_tensor("bo_sb", [128, 8], F32))
    mur_sb = A(nc.sbuf_tensor("mur_sb", [128, 8], F32))
    muk_sb = A(nc.sbuf_tensor("muk_sb", [128, 8], F32))

    pacc01 = A(nc.psum_tensor("pacc01", [128, 1024], F32))
    sg0 = A(nc.psum_tensor("sg0", [128, 1024], F32))
    sg1 = A(nc.psum_tensor("sg1", [128, 1024], F32))
    sg2 = A(nc.psum_tensor("sg2", [128, 1024], F32))

    def PA(i):
        return pacc01[:, (i % 2) * 512:((i % 2) + 1) * 512]
    sgs = [sg0, sg1, sg2]

    sem_names = ["g_x", "g_x2", "g_w0", "g_w1", "g_w2", "g_w3", "s_ld", "s_xr0",
                 "s_xr1", "s_st0", "s_st1", "t_mm", "v_dn", "sc_dn"]
    sems = {n: A(nc.semaphore(n)) for n in sem_names}
    block = A(nc.Block())

    qsched = []
    for s in range(NSEQ):
        for qi in range(4):
            qsched.append(QW["wq"] + qi)
            qsched.append(QW["wka"] + qi)
        for b in range(NBLK):
            for m in ("wvi", "wo", "wfk"):
                for qi in range(4):
                    qsched.append(QW[m] + qi)
            for qi in range(4):
                qsched.append(QW["wr"] + qi)
                qsched.append(QW["wfv"] + qi)
    NQS = len(qsched)

    def walk(en, engs):
        w = Walk(nc, en, engs, sems)
        slot_loads = [0, 0, 0]
        slot_free_t = [0, 0, 0]
        qpos = 0

        def load_quarter(i):
            nonlocal qpos
            assert i == qpos
            qpos += 1
            slot = i % 3
            src_q = qsched[i]
            w.wait("gpsimd", "t_mm", slot_free_t[slot])
            n = w.op("gpsimd",
                     lambda g, slot=slot, src_q=src_q: g.dma_start(
                         wbuf[:, slot * 2048:(slot + 1) * 2048], wpk[src_q, :, :]),
                     inc=f"g_w{slot}", inc_amt=16)
            slot_loads[slot] = n

        def use_quarter(i, last_mm_cnt):
            slot = i % 3
            slot_free_t[slot] = last_mm_cnt
            if qpos == i + 3 and qpos < NQS:
                load_quarter(i + 3)

        def wslot(i):
            return i % 3

        def wtile(i, dcl, cc):
            sl = wslot(i)
            return wbuf[:, sl * 2048 + dcl * 1024 + cc * 128:
                        sl * 2048 + dcl * 1024 + cc * 128 + 128]

        for fn in (lambda e: e.dma_start(ones_sb[:], ones_in[:, :]),
                   lambda e: e.dma_start(ident_sb[:], ident_in[:, :]),
                   lambda e: e.dma_start(indic_sb[:], indic_in[:, :]),
                   lambda e: e.dma_start(bqkv_sb[:], bqkv[:, :]),
                   lambda e: e.dma_start(bo_sb[:], bo_in[:, :]),
                   lambda e: e.dma_start(mur_sb[:], mur_in[:, :]),
                   lambda e: e.dma_start(muk_sb[:], muk_in[:, :])):
            w.op("sync", fn, inc="s_ld", inc_amt=16)
        n_const = w.cnt["s_ld"]

        for i in range(3):
            load_quarter(i)

        w.op("vector", lambda e: e.memset(eps_sb[:, :], float(EPS)), inc="v_dn")
        w.op("vector", lambda e: e.memset(invT_sb[:, :], 1.0 / T), inc="v_dn")

        pacc_evt = [[("v_dn", 0)], [("v_dn", 0)]]

        def wait_pacc(engine, i):
            for sem_, val_ in pacc_evt[i]:
                w.wait(engine, sem_, val_)

        def set_pacc(i, sem_, val_):
            pacc_evt[i] = [(sem_, val_)]
        x2slots_free = [0, 0]
        sg_free_sc = [0, 0, 0]
        sg_free_v = [0, 0, 0]
        zfree_v = [0, 0, 0, 0]
        sg_rr = 0
        acol_free_v = [0, 0, 0, 0]
        acol_free_t = [0, 0, 0, 0]
        rt_free_v = 0
        rpsum_free_sc = 0
        last_ar1_reader_v = 0
        last_ar1_reader_t = 0
        ln_free_v = 0
        ln_free_t = 0
        att_last_t = 0
        xr_slot_free = [[("v_dn", 0)], [("v_dn", 0)]]
        out_slot_free = [0, 0]
        xr_loads = [0, 0]
        x_cast_cnt = 0

        carry = None

        def xr_issue(s, b, do, slot):
            for sem_, val_ in xr_slot_free[slot]:
                w.wait("sync", sem_, val_)
            n = w.op("sync",
                     lambda e, s=s, b=b, do=do, slot=slot: e.dma_start(
                         xr_stage[:, slot * 512:(slot + 1) * 512],
                         x_cm[s, do * 128:(do + 1) * 128, b * 512:(b + 1) * 512]),
                     inc=f"s_xr{slot}", inc_amt=16)
            xr_loads[slot] = n

        ln_fill_v = [0] * 4

        def gen_s2a(si):
            w.wait("gpsimd", "t_mm", att_last_t)
            cast0 = w.op(
                "gpsimd",
                lambda g, si=si: g.dma_start(
                    bass.AP(qb, 0, [[16384, 128], [2048, 4], [1, 2048]]),
                    bass.AP(x_cm, si * C * T, [[2048, 128], [128 * 2048, 4], [1, 2048]])),
                inc="g_x", inc_amt=16)
            cast1 = w.op(
                "gpsimd",
                lambda g, si=si: g.dma_start(
                    bass.AP(qb, 4 * 2048, [[16384, 128], [2048, 4], [1, 2048]]),
                    bass.AP(x_cm, si * C * T + 4 * 128 * 2048,
                            [[2048, 128], [128 * 2048, 4], [1, 2048]])),
                inc="g_x2", inc_amt=16)
            yield 2

            def emit_ln(tt):
                w.wait("vector", "t_mm", x2slots_free[0])
                w.wait("vector", "t_mm", x2slots_free[1])
                lnv = None
                w.wait("vector", "t_mm", ln_free_t)
                for cc in range(NCH):
                    sl0 = (cc % 2) * 512
                    w.op("vector",
                         lambda e, cc=cc, tt=tt, sl0=sl0: e.tensor_tensor(
                             e_scr[:, sl0:sl0 + 512],
                             qb[:, cc * 2048 + tt * 512: cc * 2048 + (tt + 1) * 512],
                             ab2_bf[:, :], op=OP.mult),
                         inc="v_dn", lag=2)
                    lnv = w.op("vector",
                               lambda e, cc=cc, tt=tt, sl0=sl0: e.tensor_tensor(
                                   ln[:, cc * 2048 + tt * 512: cc * 2048 + (tt + 1) * 512],
                                   e_scr[:, sl0:sl0 + 512],
                                   mb2_bf[:, :], op=OP.subtract),
                               inc="v_dn")
                ln_fill_v[tt] = lnv
                x2slots_free[0] = 0

            for tt in range(4):
                sg = tt % 3
                w.wait("tensor", "s_ld", n_const)
                w.wait("tensor", "sc_dn", sg_free_sc[sg])
                w.wait("tensor", "v_dn", sg_free_v[sg])
                mm_s1 = None
                for cc in range(NCH):
                    w.wait("tensor", "g_x" if cc < 4 else "g_x2",
                           cast0 if cc < 4 else cast1)
                    mm_s1 = w.op("tensor",
                                 lambda e, cc=cc, tt=tt, sg=sg: e.matmul(
                                     sgs[sg][:, 0:512],
                                     ones_sb[:, :],
                                     qb[:, cc * 2048 + tt * 512: cc * 2048 + (tt + 1) * 512],
                                     start=(cc == 0), stop=(cc == NCH - 1)),
                                 inc="t_mm" if cc == NCH - 1 else None)
                mm_s2 = None
                for cc in range(NCH):
                    xsl = cc % 2
                    w.wait("vector", "g_x" if cc < 4 else "g_x2",
                           cast0 if cc < 4 else cast1)
                    w.wait("vector", "sc_dn", max(sg_free_sc))
                    w.wait("vector", "t_mm", x2slots_free[xsl])
                    x2v = w.op("vector",
                               lambda e, cc=cc, tt=tt, xsl=xsl: e.tensor_tensor(
                                   e_scr[:, xsl * 512:(xsl + 1) * 512],
                                   qb[:, cc * 2048 + tt * 512: cc * 2048 + (tt + 1) * 512],
                                   qb[:, cc * 2048 + tt * 512: cc * 2048 + (tt + 1) * 512],
                                   op=OP.mult),
                               inc="v_dn", lag=1)
                    w.wait("tensor", "v_dn", x2v)
                    mm_s2 = w.op("tensor",
                                 lambda e, cc=cc, xsl=xsl, sg=sg: e.matmul(
                                     sgs[sg][:, 512:1024],
                                     ones_sb[:, :],
                                     e_scr[:, xsl * 512:(xsl + 1) * 512],
                                     start=(cc == 0), stop=(cc == NCH - 1)),
                                 inc="t_mm")
                    x2slots_free[xsl] = mm_s2
                yield 5
                w.wait("vector", "t_mm", mm_s1)
                pmv = w.op("vector", lambda e, sg=sg: e.tensor_copy(
                    xr_stage[:, 0:512], sgs[sg][:, 0:512]), inc="v_dn", lag=1)
                w.wait("vector", "t_mm", mm_s2)
                pav = w.op("vector", lambda e, sg=sg: e.tensor_copy(
                    xr_stage[:, 512:1024], sgs[sg][:, 512:1024]), inc="v_dn", lag=1)
                sg_free_v[sg] = pav
                w.op("vector", lambda e: e.tensor_tensor(
                    t32_s[:, :], xr_stage[:, 0:512], xr_stage[:, 0:512],
                    op=OP.mult), inc="v_dn", lag=1)
                vv = w.op("vector", lambda e: e.tensor_tensor(
                    xr_stage[:, 512:1024], xr_stage[:, 512:1024], t32_s[:, :],
                    op=OP.subtract), inc="v_dn")
                w.wait("scalar", "v_dn", vv)
                w.op("scalar", lambda e: e.activation(
                    xr_stage[:, 512:1024], xr_stage[:, 512:1024],
                    AF.Ln, bias=eps_sb[:, :]), inc="sc_dn")
                sc_a = w.op("scalar", lambda e: e.activation(
                    t32_s[:, :], xr_stage[:, 512:1024], AF.Exp, scale=-0.5), inc="sc_dn")
                if tt > 0:
                    emit_ln(tt - 1)
                w.wait("vector", "sc_dn", sc_a)
                w.op("vector", lambda e: e.tensor_tensor(
                    xr_stage[:, 0:512], xr_stage[:, 0:512], t32_s[:, :],
                    op=OP.mult), inc="v_dn")
                w.op("vector", lambda e: e.tensor_copy(ab2_bf[:, :], t32_s[:, :]), inc="v_dn", lag=1)
                mb2c = w.op("vector", lambda e: e.tensor_copy(
                    mb2_bf[:, :], xr_stage[:, 0:512]), inc="v_dn")
                yield 5
            emit_ln(3)
            yield 3
            xr_slot_free[0] = [("v_dn", mb2c)]
            xr_slot_free[1] = [("sc_dn", sc_a), ("v_dn", vv)]

        for s in range(NSEQ):
            sq = s * (8 + NBLK * 20)

            g2 = gen_s2a(s)
            if carry is not None:
                a3 = 0
                b3 = 0
                for c2 in g2:
                    a3 += c2
                    while carry is not None and b3 * 10 < a3 * 22:
                        try:
                            v = next(carry)
                        except StopIteration:
                            carry = None
                            break
                        if v >= 0:
                            b3 += v
                while carry is not None:
                    try:
                        next(carry)
                    except StopIteration:
                        carry = None
            else:
                for _ in g2:
                    pass

            kc_done_ch = [0] * NCH


            def emit_qkv_quarter(qi, which):
                gq = sq + qi * 2 + which
                slot = wslot(gq)
                is_k = which == 1
                dbase = qi * 2
                kevt = {}
                last_mm = 0
                for dcl in range(2):
                    d = dbase + dcl
                    for j in range(4):
                        tt = j
                        half = j % 2
                        w.wait("tensor", f"g_w{slot}", slot_loads[slot])
                        wait_pacc("tensor", half)
                        w.wait("tensor", "sc_dn", rpsum_free_sc)
                        w.wait("tensor", "v_dn", ln_fill_v[tt])
                        mmn = None
                        for cc in range(NCH):
                            mmn = w.op("tensor",
                                       lambda e, gq=gq, dcl=dcl, cc=cc, tt=tt, half=half: e.matmul(
                                           PA(half),
                                           wtile(gq, dcl, cc),
                                           ln[:, cc * 2048 + tt * 512: cc * 2048 + (tt + 1) * 512],
                                           start=(cc == 0), stop=(cc == NCH - 1)),
                                       inc="t_mm" if cc == NCH - 1 else None)
                        last_mm = mmn
                        use_sc = j % 2 == 0
                        dst = kb if is_k else qb
                        bcol = 8 + d if is_k else d
                        dsl = d * 2048 + j * 512
                        kslot = 4 * d + j
                        if use_sc:
                            w.wait("scalar", "t_mm", mmn)
                            if is_k:
                                ev = w.op("scalar",
                                          lambda e, dst=dst, dsl=dsl, bcol=bcol, kslot=kslot, half=half: e.activation(
                                              dst[:, dsl:dsl + 512],
                                              PA(half), AF.Identity,
                                              bias=bqkv_sb[:, bcol:bcol + 1],
                                              accum_out=kbar[:, kslot:kslot + 1]),
                                          inc="sc_dn", lag=1)
                            else:
                                ev = w.op("scalar",
                                          lambda e, dst=dst, dsl=dsl, bcol=bcol, half=half: e.activation(
                                              dst[:, dsl:dsl + 512],
                                              PA(half), AF.Identity,
                                              bias=bqkv_sb[:, bcol:bcol + 1]),
                                          inc="sc_dn", lag=1)
                            set_pacc(half, "sc_dn", ev)
                            kevt[j] = ("sc_dn", ev)
                        else:
                            w.wait("vector", "t_mm", mmn)
                            if is_k:
                                ev = w.op("vector",
                                          lambda e, dst=dst, dsl=dsl, bcol=bcol, kslot=kslot, half=half: e.tensor_scalar(
                                              dst[:, dsl:dsl + 512],
                                              PA(half), bqkv_sb[:, bcol:bcol + 1], 0.0,
                                              op0=OP.add, op1=OP.add,
                                              accum_out=kbar[:, kslot:kslot + 1]),
                                          inc="v_dn", lag=1)
                            else:
                                ev = w.op("vector",
                                          lambda e, dst=dst, dsl=dsl, bcol=bcol, half=half: e.tensor_scalar(
                                              dst[:, dsl:dsl + 512],
                                              PA(half), bqkv_sb[:, bcol:bcol + 1], 0.0,
                                              op0=OP.add, op1=OP.add),
                                          inc="v_dn", lag=1)
                            set_pacc(half, "v_dn", ev)
                            kevt[j] = ("v_dn", ev)
                    if is_k:
                        for j in range(4):
                            w.wait("vector", kevt[j][0], kevt[j][1])
                        w.op("vector", lambda e, d=d: e.tensor_tensor(
                            kbar[:, 32 + d:33 + d], kbar[:, 4 * d:4 * d + 1],
                            kbar[:, 4 * d + 1:4 * d + 2], op=OP.add), inc="v_dn")
                        w.op("vector", lambda e, d=d: e.tensor_tensor(
                            mean_s[:, 0:1], kbar[:, 4 * d + 2:4 * d + 3],
                            kbar[:, 4 * d + 3:4 * d + 4], op=OP.add), inc="v_dn")
                        w.op("vector", lambda e, d=d: e.tensor_tensor(
                            kbar[:, 32 + d:33 + d], kbar[:, 32 + d:33 + d],
                            mean_s[:, 0:1], op=OP.add), inc="v_dn")
                        w.op("vector", lambda e, d=d: e.tensor_scalar(
                            kbar[:, 32 + d:33 + d], kbar[:, 32 + d:33 + d],
                            invT_sb[:, 0:1], None, op0=OP.mult), inc="v_dn")
                        kc_done_ch[d] = w.op("vector",
                                             lambda e, d=d: e.tensor_scalar(
                                                 kb[:, d * 2048:(d + 1) * 2048],
                                                 kb[:, d * 2048:(d + 1) * 2048],
                                                 kbar[:, 32 + d:33 + d], None,
                                                 op0=OP.subtract),
                                             inc="v_dn")
                use_quarter(gq, last_mm)

            def emit_tile(p, lc, mh):
                nonlocal sg_rr
                bb = lc // 4
                gA = sg_rr % 3
                gB = (sg_rr + 1) % 3
                sg_rr += 2
                w.wait("tensor", "sc_dn", sg_free_sc[gA])
                w.wait("tensor", "sc_dn", sg_free_sc[gB])
                w.wait("tensor", "v_dn", sg_free_v[gA])
                w.wait("tensor", "v_dn", sg_free_v[gB])
                w.wait("tensor", "v_dn", kc_done_ch[p])
                mm_fill = [0, 0]
                for n in range(2):
                    for ab in range(2):
                        r0 = ab * 64
                        g = gA if ab == 0 else gB
                        mm_fill[ab] = w.op(
                            "tensor",
                            lambda e, p=p, lc=lc, mh=mh, n=n, r0=r0, g=g: e.matmul(
                                sgs[g][:, n * 512:(n + 1) * 512],
                                qb[r0:r0 + 64,
                                   p * 2048 + lc * 128: p * 2048 + (lc + 1) * 128],
                                kb[r0:r0 + 64,
                                   p * 2048 + mh * 1024 + n * 512:
                                   p * 2048 + mh * 1024 + (n + 1) * 512],
                                start=True, stop=True,
                                tile_position=(r0, 0)),
                            inc="t_mm" if n == 1 else None)
                for ab in range(2):
                    g = gA if ab == 0 else gB
                    zt = z0_sb if mh == 0 else z1_sb
                    col = lc * 16 + 2 * p + ab
                    w.wait("scalar", "t_mm", mm_fill[ab])
                    w.wait("scalar", "v_dn", zfree_v[bb])
                    sg_free_sc[g] = w.op(
                        "scalar",
                        lambda e, ab=ab, col=col, g=g, zt=zt: e.activation(
                            e_scr[:, ab * 1024:(ab + 1) * 1024],
                            sgs[g][:, :], AF.Exp,
                            scale=float(SCALE),
                            accum_out=zt[:, col:col + 1]),
                        inc="sc_dn", lag=1)

            for qi in range(4):
                emit_qkv_quarter(qi, 0)
                emit_qkv_quarter(qi, 1)
                for p in (2 * qi, 2 * qi + 1):
                    for lci in range(4):
                        for mh in range(2):
                            emit_tile(p, 0 * 4 + lci, mh)

            kc_done = max(kc_done_ch)

            def gen_attn(b, kc_done):
                for lci in range(4):
                    lc = b * 4 + lci
                    for p in range(NCH):
                        for mh in range(2):
                            emit_tile(p, lc, mh)
                            yield 4

            def gen_post(b, qbase, si):
                nonlocal last_ar1_reader_v, last_ar1_reader_t, rt_free_v, rpsum_free_sc
                w.wait("vector", "sc_dn", max(sg_free_sc))
                w.wait("vector", "v_dn", acol_free_v[b])
                w.wait("vector", "t_mm", acol_free_t[b])
                zfree_v[b] = w.op("vector", lambda e, b=b: e.tensor_tensor(
                    alpha_sb[:, b * 64:(b + 1) * 64],
                    z0_sb[:, b * 64:(b + 1) * 64],
                    z1_sb[:, b * 64:(b + 1) * 64], op=OP.add), inc="v_dn")
                w.op("vector", lambda e, b=b: e.reciprocal(
                    var_s[:, 0:64], alpha_sb[:, b * 64:(b + 1) * 64]), inc="v_dn")
                amul = w.op("vector", lambda e, b=b: e.tensor_tensor(
                    alpha_sb[:, b * 64:(b + 1) * 64],
                    alpha_sb[:, b * 64:(b + 1) * 64],
                    var_s[:, 0:64], op=OP.mult), inc="v_dn")
                acol_free_v[b] = amul
                yield 6
                w.wait("tensor", "v_dn", amul)
                wait_pacc("tensor", 0)
                tcnt = 0
                for lci in range(4):
                    lc = b * 4 + lci
                    tcnt = w.op("tensor",
                                lambda e, lc=lc, lci=lci: e.transpose(
                                    pacc01[0:16, lci * 128:(lci + 1) * 128],
                                    alpha_sb[:, lc * 16:(lc + 1) * 16],
                                    ident_sb[:, :]),
                                inc="t_mm" if lci == 3 else None)
                acol_free_t[b] = tcnt
                w.wait("vector", "t_mm", tcnt)
                atv = w.op("vector",
                           lambda e: e.tensor_copy(
                               alphaT[0:16, :], pacc01[0:16, 0:512]),
                           inc="v_dn", lag=1)
                set_pacc(0, "v_dn", atv)
                alphaT_ready_v = atv
                yield 6

                for do in range(2):
                    xr_issue(si, b, do, do % 2)
                vop = 0
                for dv in range(NCH):
                    gq = qbase + (dv // 2)
                    slot = wslot(gq)
                    dcl = dv % 2
                    pa = dv % 2
                    pb = 1 - pa
                    w.wait("tensor", f"g_w{slot}", slot_loads[slot])
                    wait_pacc("tensor", pa)
                    mmv = None
                    for cc in range(NCH):
                        mmv = w.op("tensor",
                                   lambda e, gq=gq, dcl=dcl, cc=cc, b=b, pa=pa: e.matmul(
                                       PA(pa),
                                       wtile(gq, dcl, cc),
                                       ln[:, cc * 2048 + b * 512: cc * 2048 + (b + 1) * 512],
                                       start=(cc == 0), stop=(cc == NCH - 1)),
                                   inc="t_mm" if cc == NCH - 1 else None)
                    if dv % 2 == 1:
                        use_quarter(gq, mmv)
                    w.wait("tensor", "v_dn", alphaT_ready_v)
                    wait_pacc("tensor", pb)
                    abmm = w.op("tensor",
                                lambda e, dv=dv, pb=pb: e.matmul(
                                    PA(pb),
                                    indic_sb[:, dv * 128:(dv + 1) * 128],
                                    alphaT[0:16, :],
                                    start=True, stop=True),
                                inc="t_mm")
                    vsl = (dv % 2) * 512
                    w.wait("vector", "t_mm", mmv)
                    vev = w.op("vector",
                               lambda e, dv=dv, pa=pa, vsl=vsl: e.tensor_scalar(
                                   v_scr[:, vsl:vsl + 512], PA(pa),
                                   bqkv_sb[:, 16 + dv:17 + dv], 0.0,
                                   op0=OP.add, op1=OP.add),
                               inc="v_dn", lag=1)
                    set_pacc(pa, "v_dn", vev)
                    w.wait("vector", "t_mm", abmm)
                    vop = w.op("vector",
                               lambda e, dv=dv, vsl=vsl, pb=pb: e.tensor_tensor(
                                   opt[:, dv * 512:(dv + 1) * 512],
                                   v_scr[:, vsl:vsl + 512], PA(pb), op=OP.mult),
                               inc="v_dn")
                    set_pacc(pb, "v_dn", vop)
                    yield 9
                opt_fill_v = vop
                yield -1000 - b

                x1v = 0
                for do in range(NCH):
                    gq = qbase + 4 + (do // 2)
                    slot = wslot(gq)
                    dcl = do % 2
                    pa = do % 2
                    w.wait("tensor", f"g_w{slot}", slot_loads[slot])
                    wait_pacc("tensor", pa)
                    w.wait("tensor", "v_dn", opt_fill_v)
                    mmo = None
                    for dv in range(NCH):
                        mmo = w.op("tensor",
                                   lambda e, gq=gq, dcl=dcl, dv=dv, pa=pa: e.matmul(
                                       PA(pa),
                                       wtile(gq, dcl, dv),
                                       opt[:, dv * 512:(dv + 1) * 512],
                                       start=(dv == 0), stop=(dv == NCH - 1)),
                                   inc="t_mm" if dv == NCH - 1 else None)
                    if do % 2 == 1:
                        use_quarter(gq, mmo)
                    xsl = do % 2
                    w.wait("vector", "t_mm", mmo)
                    w.wait("vector", f"s_xr{xsl}", xr_loads[xsl])
                    x1v = w.op("vector",
                               lambda e, do=do, pa=pa, xsl=xsl: e.scalar_tensor_tensor(
                                   x1[:, do * 512:(do + 1) * 512],
                                   PA(pa), bo_sb[:, do:do + 1],
                                   xr_stage[:, xsl * 512:(xsl + 1) * 512],
                                   op0=OP.add, op1=OP.add),
                               inc="v_dn", lag=2)
                    set_pacc(pa, "v_dn", x1v)
                    xr_slot_free[xsl] = [("v_dn", x1v)]
                    if do + 2 < NCH:
                        xr_issue(si, b, do + 2, (do + 2) % 2)
                    yield 8

                yield -1 - b
                xbv = w.op("vector", lambda e: e.tensor_copy(
                    ar1[:, 0:4096], x1[:, :]), inc="v_dn")
                last_ar1_reader_v = xbv
                w.wait("tensor", "v_dn", xbv)
                wait_pacc("tensor", 0)
                mm_s1 = None
                for cc in range(NCH):
                    mm_s1 = w.op("tensor",
                                 lambda e, cc=cc: e.matmul(
                                     PA(0), ones_sb[:, :],
                                     ar1[:, cc * 512:(cc + 1) * 512],
                                     start=(cc == 0), stop=(cc == NCH - 1)),
                                 inc="t_mm" if cc == NCH - 1 else None)
                mm_s2 = None
                for cc in range(NCH):
                    xsl = cc % 2
                    w.wait("vector", "t_mm", x2slots_free[xsl])
                    x2v = w.op("vector",
                               lambda e, cc=cc, xsl=xsl: e.tensor_tensor(
                                   ar1[:, 8192 + xsl * 512: 8192 + (xsl + 1) * 512],
                                   ar1[:, cc * 512:(cc + 1) * 512],
                                   ar1[:, cc * 512:(cc + 1) * 512],
                                   op=OP.mult),
                               inc="v_dn", lag=(0 if cc == 0 else 1))
                    w.wait("tensor", "v_dn", x2v)
                    if cc == 0:
                        wait_pacc("tensor", 1)
                    mm_s2 = w.op("tensor",
                                 lambda e, cc=cc, xsl=xsl: e.matmul(
                                     PA(1), ones_sb[:, :],
                                     ar1[:, 8192 + xsl * 512: 8192 + (xsl + 1) * 512],
                                     start=(cc == 0), stop=(cc == NCH - 1)),
                                 inc="t_mm")
                    x2slots_free[xsl] = mm_s2
                yield 16
                w.wait("vector", "t_mm", mm_s1)
                set_pacc(0, "v_dn", w.op("vector", lambda e: e.tensor_copy(
                    mean_s[:, :], PA(0)), inc="v_dn", lag=1))
                w.wait("vector", "t_mm", mm_s2)
                set_pacc(1, "v_dn", w.op("vector", lambda e: e.tensor_copy(
                    var_s[:, :], PA(1)), inc="v_dn", lag=1))
                w.op("vector", lambda e: e.tensor_tensor(
                    t3_s[:, :], mean_s[:, :], mean_s[:, :], op=OP.mult), inc="v_dn", lag=1)
                vv = w.op("vector", lambda e: e.tensor_tensor(
                    var_s[:, :], var_s[:, :], t3_s[:, :], op=OP.subtract), inc="v_dn")
                w.wait("scalar", "v_dn", vv)
                w.op("scalar", lambda e: e.activation(
                    var_s[:, :], var_s[:, :], AF.Ln, bias=eps_sb[:, :]), inc="sc_dn")
                sc_a = w.op("scalar", lambda e: e.activation(
                    t3_s[:, :], var_s[:, :], AF.Exp, scale=-0.5), inc="sc_dn")
                w.wait("vector", "sc_dn", sc_a)
                w.op("vector", lambda e: e.tensor_tensor(
                    mean_s[:, :], mean_s[:, :], t3_s[:, :], op=OP.mult), inc="v_dn")
                w.op("vector", lambda e: e.tensor_copy(ab_bf[:, :], t3_s[:, :]), inc="v_dn", lag=1)
                w.op("vector", lambda e: e.tensor_copy(mb_bf[:, :], mean_s[:, :]), inc="v_dn")
                yield 4
                w.wait("vector", "t_mm", max(x2slots_free))
                for cc in range(NCH):
                    w.op("vector",
                         lambda e, cc=cc: e.tensor_tensor(
                             ar1[:, 8192 + cc * 512: 8192 + (cc + 1) * 512],
                             ar1[:, cc * 512:(cc + 1) * 512],
                             ab_bf[:, :], op=OP.mult),
                         inc="v_dn", lag=2)
                    w.op("vector",
                         lambda e, cc=cc: e.tensor_tensor(
                             ar1[:, 4096 + cc * 512: 4096 + (cc + 1) * 512],
                             ar1[:, 8192 + cc * 512: 8192 + (cc + 1) * 512],
                             mb_bf[:, :], op=OP.subtract),
                         inc="v_dn")
                    if cc % 2 == 1:
                        yield 3
                if b == 0:
                    w.op("vector", lambda e: e.memset(prevc[:, :], 0.0), inc="v_dn")
                yield 4
                for cc in range(NCH):
                    c0 = 4096 + cc * 512
                    dd0 = 8192 + cc * 512
                    w.op("vector",
                         lambda e, c0=c0, dd0=dd0: e.tensor_tensor(
                             bass.AP(ar1, dd0 + 1, [[16384, 128], [1, 511]]),
                             bass.AP(ar1, c0 + 1, [[16384, 128], [1, 511]]),
                             bass.AP(ar1, c0, [[16384, 128], [1, 511]]),
                             op=OP.subtract),
                         inc="v_dn", lag=2)
                w.op("vector",
                     lambda e: e.tensor_tensor(
                         bass.AP(ar1, 8192, [[16384, 128], [512, 8]]),
                         bass.AP(ar1, 4096, [[16384, 128], [512, 8]]),
                         prevc[:, 0:8], op=OP.subtract),
                     inc="v_dn", lag=1)
                yield 6
                for cc in range(NCH):
                    c0 = 4096 + cc * 512
                    dd0 = 8192 + cc * 512
                    w.op("vector",
                         lambda e, c0=c0, dd0=dd0, cc=cc: e.scalar_tensor_tensor(
                             bass.AP(ar1, 12288 + cc * 512 + 1, [[16384, 128], [1, 511]]),
                             bass.AP(ar1, dd0 + 1, [[16384, 128], [1, 511]]),
                             mur_sb[:, cc:cc + 1],
                             bass.AP(ar1, c0, [[16384, 128], [1, 511]]),
                             op0=OP.mult, op1=OP.add),
                         inc="v_dn", lag=2)
                    w.op("vector",
                         lambda e, dd0=dd0, cc=cc: e.scalar_tensor_tensor(
                             ar1[:, 12288 + cc * 512: 12288 + cc * 512 + 1],
                             ar1[:, dd0:dd0 + 1], mur_sb[:, cc:cc + 1],
                             prevc[:, cc:cc + 1], op0=OP.mult, op1=OP.add),
                         inc="v_dn")
                    w.op("vector",
                         lambda e, c0=c0, dd0=dd0, cc=cc: e.scalar_tensor_tensor(
                             bass.AP(ar6, cc * 512 + 1, [[4096, 128], [1, 511]]),
                             bass.AP(ar1, dd0 + 1, [[16384, 128], [1, 511]]),
                             muk_sb[:, cc:cc + 1],
                             bass.AP(ar1, c0, [[16384, 128], [1, 511]]),
                             op0=OP.mult, op1=OP.add),
                         inc="v_dn", lag=2)
                    w.op("vector",
                         lambda e, dd0=dd0, cc=cc: e.scalar_tensor_tensor(
                             ar6[:, cc * 512: cc * 512 + 1],
                             ar1[:, dd0:dd0 + 1], muk_sb[:, cc:cc + 1],
                             prevc[:, cc:cc + 1], op0=OP.mult, op1=OP.add),
                         inc="v_dn")
                    if cc % 2 == 1:
                        yield 4
                mixv = w.op("vector", lambda e: e.tensor_copy(
                    prevc[:, :],
                    bass.AP(ar1, 4096 + 511, [[16384, 128], [512, 8]])), inc="v_dn")
                last_ar1_reader_v = mixv
                yield 8

                kav = 0
                for dd in range(NCH):
                    gq = qbase + 8 + (dd // 2)
                    slot = wslot(gq)
                    dcl = dd % 2
                    pa = dd % 2
                    w.wait("tensor", f"g_w{slot}", slot_loads[slot])
                    w.wait("tensor", "v_dn", mixv)
                    wait_pacc("tensor", pa)
                    mmk = None
                    for cc in range(NCH):
                        mmk = w.op("tensor",
                                   lambda e, gq=gq, dcl=dcl, cc=cc, pa=pa: e.matmul(
                                       PA(pa),
                                       wtile(gq, dcl, cc),
                                       ar6[:, cc * 512:(cc + 1) * 512],
                                       start=(cc == 0), stop=(cc == NCH - 1)),
                                   inc="t_mm" if cc == NCH - 1 else None)
                    if dd % 2 == 1:
                        use_quarter(gq, mmk)
                    w.wait("vector", "t_mm", mmk)
                    kasc = var_s if dd % 2 == 0 else mean_s
                    w.op("vector",
                         lambda e, pa=pa, kasc=kasc: e.tensor_scalar(
                             kasc[:, :], PA(pa), 0.0, None, op0=OP.max),
                         inc="v_dn", lag=2)
                    kav = w.op("vector",
                               lambda e, dd=dd, kasc=kasc: e.tensor_tensor(
                                   ar1[:, 8192 + dd * 512: 8192 + (dd + 1) * 512],
                                   kasc[:, :], kasc[:, :], op=OP.mult),
                               inc="v_dn")
                    set_pacc(pa, "v_dn", kav)
                    yield 8
                last_ar1_reader_v = kav

                mmf = 0
                for dd in range(NCH):
                    gqr = qbase + 12 + 2 * (dd // 2)
                    gqv = gqr + 1
                    dcl = dd % 2
                    sr = wslot(gqr)
                    sv = wslot(gqv)
                    pfr = dd % 2
                    pff = 1 - pfr
                    w.wait("tensor", f"g_w{sr}", slot_loads[sr])
                    w.wait("tensor", "v_dn", kav)
                    wait_pacc("tensor", pfr)
                    w.wait("tensor", "sc_dn", rpsum_free_sc)
                    mmr = None
                    for cc in range(NCH):
                        mmr = w.op("tensor",
                                   lambda e, gqr=gqr, dcl=dcl, cc=cc, pfr=pfr: e.matmul(
                                       PA(pfr),
                                       wtile(gqr, dcl, cc),
                                       ar1[:, 12288 + cc * 512: 12288 + (cc + 1) * 512],
                                       start=(cc == 0), stop=(cc == NCH - 1)),
                                   inc="t_mm" if cc == NCH - 1 else None)
                    yield 8
                    w.wait("tensor", f"g_w{sv}", slot_loads[sv])
                    wait_pacc("tensor", pff)
                    mmf = None
                    for cc in range(NCH):
                        mmf = w.op("tensor",
                                   lambda e, gqv=gqv, dcl=dcl, cc=cc, pff=pff: e.matmul(
                                       PA(pff),
                                       wtile(gqv, dcl, cc),
                                       ar1[:, 8192 + cc * 512: 8192 + (cc + 1) * 512],
                                       start=(cc == 0), stop=(cc == NCH - 1)),
                                   inc="t_mm" if cc == NCH - 1 else None)
                    if dd % 2 == 1:
                        use_quarter(gqr, mmf)
                        use_quarter(gqv, mmf)
                    w.wait("scalar", "t_mm", mmr)
                    w.wait("scalar", "v_dn", rt_free_v)
                    rpsum_free_sc = w.op("scalar",
                         lambda e, pfr=pfr: e.activation(
                             rt_scr[:, :], PA(pfr), AF.Exp, scale=-1.0),
                         inc="sc_dn", lag=1)
                    w.op("scalar", lambda e: e.activation(
                        rt_scr[:, :], rt_scr[:, :], AF.Ln, bias=1.0), inc="sc_dn")
                    scs = w.op("scalar", lambda e: e.activation(
                        rt_scr[:, :], rt_scr[:, :], AF.Exp, scale=-1.0), inc="sc_dn")
                    osl = dd % 2
                    w.wait("vector", "sc_dn", scs)
                    w.wait("vector", "t_mm", mmf)
                    w.wait("vector", f"s_st{osl}", out_slot_free[osl])
                    fsc = t3_s if dd % 2 == 0 else mean_s
                    t1v = w.op("vector",
                               lambda e, pff=pff, fsc=fsc: e.tensor_tensor(
                                   fsc[:, :], PA(pff), rt_scr[:, :], op=OP.mult),
                               inc="v_dn", lag=1)
                    rt_free_v = t1v
                    set_pacc(pff, "v_dn", t1v)
                    ov = w.op("vector",
                              lambda e, dd=dd, osl=osl, fsc=fsc: e.tensor_tensor(
                                  out_stage[:, osl * 512:(osl + 1) * 512],
                                  fsc[:, :], x1[:, dd * 512:(dd + 1) * 512],
                                  op=OP.add),
                              inc="v_dn")
                    w.wait("sync", "v_dn", ov)
                    stn = w.op("sync",
                               lambda e, si=si, b=b, dd=dd, osl=osl: e.dma_start(
                                   out_cm[si, dd * 128:(dd + 1) * 128,
                                          b * 512:(b + 1) * 512],
                                   out_stage[:, osl * 512:(osl + 1) * 512]),
                               inc=f"s_st{osl}", inc_amt=16)
                    out_slot_free[osl] = stn
                    yield 8
                last_ar1_reader_t = w.cnt["t_mm"]

            for b in range(NBLK):
                a_cost = 0
                b_cost = 0
                if b > 0:
                    ga = gen_attn(b, kc_done)
                    for c in ga:
                        a_cost += c
                        while carry is not None and b_cost * 20 < max(0, a_cost - 8) * 27:
                            try:
                                v = next(carry)
                                if v >= 0:
                                    b_cost += v
                            except StopIteration:
                                carry = None
                if b == NBLK - 1:
                    att_last_t = w.cnt["t_mm"]
                ng = gen_post(b, sq + 8 + b * 20, s)
                carry = ng if carry is None else chain(carry, ng)
            if s == NSEQ - 1:
                for _ in carry:
                    pass
                carry = None
            else:
                for c in carry:
                    if c == -1 - (NBLK - 1):
                        break
                ln_free_t = w.cnt["t_mm"]

        w.wait("sync", "s_st0", w.cnt["s_st0"])
        w.wait("sync", "s_st1", w.cnt["s_st1"])
        return w

    @block.tensor
    def _(e):
        walk("tensor", {"tensor": e})

    @block.vector
    def _(e):
        walk("vector", {"vector": e})

    @block.scalar
    def _(e):
        walk("scalar", {"scalar": e})

    @block.gpsimd
    def _(e):
        walk("gpsimd", {"gpsimd": e})

    @block.sync
    def _(e):
        walk("sync", {"sync": e})

    ctx.close()
    return nc


def _pack_quarters(MT):
    t = MT.reshape(8, 128, 8, 128)
    out = np.empty((4, 128, 2048), np.float32)
    for qi in range(4):
        blk = t[:, :, 2 * qi:2 * qi + 2, :]
        out[qi] = blk.transpose(1, 2, 0, 3).reshape(128, 2048)
    return out


def _host_prep(inputs):
    w_in = np.asarray(inputs["L0_attn_w_in"], np.float32)
    b_in = np.asarray(inputs["L0_attn_b_in"], np.float32)
    perm_q = [h * 192 + i for h in range(NH) for i in range(64)]
    perm_k = [h * 192 + 64 + i for h in range(NH) for i in range(64)]
    perm_v = [h * 192 + 128 + i for h in range(NH) for i in range(64)]
    wqT = np.ascontiguousarray(w_in[perm_q].T)
    wkT = np.ascontiguousarray(w_in[perm_k].T)
    wvT = np.ascontiguousarray(w_in[perm_v].T)
    woT = np.ascontiguousarray(np.asarray(inputs["L0_attn_w_out"], np.float32).T)
    wrT = np.ascontiguousarray(np.asarray(inputs["L0_ffn_W_r"], np.float32).T)
    wfkT = np.ascontiguousarray(np.asarray(inputs["L0_ffn_W_k"], np.float32).T)
    wfvT = np.ascontiguousarray(np.asarray(inputs["L0_ffn_W_v"], np.float32).T)

    wpk = np.empty((NQ, 128, 2048), np.float32)
    wpk[QW["wq"]:QW["wq"] + 4] = _pack_quarters(wqT)
    wpk[QW["wka"]:QW["wka"] + 4] = _pack_quarters(wkT)
    wpk[QW["wvi"]:QW["wvi"] + 4] = _pack_quarters(wvT)
    wpk[QW["wo"]:QW["wo"] + 4] = _pack_quarters(woT)
    wpk[QW["wr"]:QW["wr"] + 4] = _pack_quarters(wrT)
    wpk[QW["wfk"]:QW["wfk"] + 4] = _pack_quarters(wfkT)
    wpk[QW["wfv"]:QW["wfv"] + 4] = _pack_quarters(wfvT)

    bqkv = np.empty((128, 24), np.float32)
    bqkv[:, 0:8] = b_in[perm_q].reshape(8, 128).T
    bqkv[:, 8:16] = b_in[perm_k].reshape(8, 128).T
    bqkv[:, 16:24] = b_in[perm_v].reshape(8, 128).T
    bo = np.ascontiguousarray(
        np.asarray(inputs["L0_attn_b_out"], np.float32).reshape(8, 128).T)
    mur = np.ascontiguousarray(
        np.asarray(inputs["L0_ffn_mu_r"], np.float32).reshape(8, 128).T)
    muk = np.ascontiguousarray(
        np.asarray(inputs["L0_ffn_mu_k"], np.float32).reshape(8, 128).T)

    ones128 = np.full((128, 128), 1.0 / C, ml_dtypes.bfloat16)
    ident = np.eye(128, dtype=np.float32)
    indic = np.zeros((16, 1024), ml_dtypes.bfloat16)
    for dv in range(8):
        for p in range(128):
            indic[2 * dv + p // 64, dv * 128 + p] = 1.0
    return dict(wpk=wpk, bqkv=bqkv, bo_in=bo, mur_in=mur, muk_in=muk,
                ones_in=ones128, ident_in=ident, indic_in=indic)


def kernel(**inputs):
    x = np.asarray(inputs["L0_attn_x"], np.float32)
    shared = _host_prep(inputs)
    nc = build_program()
    in_maps = []
    for core in range(8):
        xs = x[core * NSEQ:(core + 1) * NSEQ]
        x_cmv = np.ascontiguousarray(xs.transpose(0, 2, 1))
        m = dict(shared)
        m["x_cm"] = x_cmv
        in_maps.append(m)
    res = run_bass_kernel_spmd(nc, in_maps, core_ids=list(range(8)))
    outs = [r["out_cm"] for r in res.results]
    out = np.concatenate(outs, axis=0).transpose(0, 2, 1)
    return np.ascontiguousarray(out).astype(np.float32)


if __name__ == "__main__":
    nc = build_program()
    print("built OK")


# version 31
# speedup vs baseline: 1.0205x; 1.0128x over previous
import numpy as np
import ml_dtypes
import concourse.bass as bass
import concourse.mybir as mybir
from concourse.bass_utils import run_bass_kernel_spmd
from contextlib import ExitStack
from itertools import chain

F32 = mybir.dt.float32
BF16 = mybir.dt.bfloat16
AF = mybir.ActivationFunctionType
OP = mybir.AluOpType

NH, DH, C, T = 16, 64, 1024, 2048
NCH = 8
NSEQ = 2
NBLK = 4
BT = 512
NLC = 16
SCALE = 0.125
EPS = 1e-5

QW = {"wq": 0, "wka": 4, "wvi": 8, "wo": 12, "wr": 16, "wfk": 20, "wfv": 24}
NQ = 28

ENGINES = ("tensor", "vector", "scalar", "gpsimd", "sync")


class Walk:

    def __init__(self, nc, en, eng_map, sems):
        self.nc = nc
        self.en = en
        self.eng = eng_map
        self.sems = sems
        self.cnt = {k: 0 for k in sems}
        self.hw = {}

    def wait(self, engine, sem, val):
        if val is None or val <= 0:
            return
        if self.en == engine:
            key = sem
            if self.hw.get(key, 0) >= val:
                return
            self.hw[key] = val
            self.eng[engine].wait_ge(self.sems[sem], val)

    OWN = {"vector": "v_dn", "scalar": "sc_dn"}

    def op(self, engine, fn, inc=None, inc_amt=1, lag=0):
        own = self.OWN.get(engine)
        if own is not None:
            self.wait(engine, own, self.cnt[own] - lag)
        if self.en == engine:
            inst = fn(self.eng[engine])
            if inc is not None:
                inst.then_inc(self.sems[inc], inc_amt)
        if inc is not None:
            self.cnt[inc] += inc_amt
        return self.cnt[inc] if inc is not None else None


def build_program():
    nc = bass.Bass("TRN2", target_bir_lowering=False)

    x_cm = nc.dram_tensor("x_cm", [NSEQ, C, T], F32, kind="ExternalInput")
    wpk = nc.dram_tensor("wpk", [NQ, 128, 2048], F32, kind="ExternalInput")
    bqkv = nc.dram_tensor("bqkv", [128, 24], F32, kind="ExternalInput")
    bo_in = nc.dram_tensor("bo_in", [128, 8], F32, kind="ExternalInput")
    mur_in = nc.dram_tensor("mur_in", [128, 8], F32, kind="ExternalInput")
    muk_in = nc.dram_tensor("muk_in", [128, 8], F32, kind="ExternalInput")
    ones_in = nc.dram_tensor("ones_in", [128, 128], BF16, kind="ExternalInput")
    ident_in = nc.dram_tensor("ident_in", [128, 128], F32, kind="ExternalInput")
    indic_in = nc.dram_tensor("indic_in", [16, 1024], BF16, kind="ExternalInput")
    out_cm = nc.dram_tensor("out_cm", [NSEQ, C, T], F32, kind="ExternalOutput")

    ctx = ExitStack()
    A = ctx.enter_context
    ar1 = A(nc.sbuf_tensor("ar1", [128, 16384], BF16))
    ln = A(nc.sbuf_tensor("ln", [128, 16384], BF16))
    qb = A(nc.sbuf_tensor("qb", [128, 16384], BF16))
    kb = A(nc.sbuf_tensor("kb", [128, 16384], BF16))
    ar6 = A(nc.sbuf_tensor("ar6", [128, 4096], BF16))
    x1 = A(nc.sbuf_tensor("x1", [128, 4096], F32))
    wbuf = A(nc.sbuf_tensor("wbuf", [128, 6144], BF16))
    opt = A(nc.sbuf_tensor("opt", [128, 4096], BF16))
    alphaT = A(nc.sbuf_tensor("alphaT", [16, 512], BF16))
    e_scr = A(nc.sbuf_tensor("e_scr", [128, 2048], BF16))
    xr_stage = A(nc.sbuf_tensor("xr_stage", [128, 1024], F32))
    out_stage = A(nc.sbuf_tensor("out_stage", [128, 1024], F32))
    v_scr = A(nc.sbuf_tensor("v_scr", [128, 1024], BF16))
    mean_s = A(nc.sbuf_tensor("mean_s", [128, 512], F32))
    var_s = A(nc.sbuf_tensor("var_s", [128, 512], F32))
    t3_s = A(nc.sbuf_tensor("t3_s", [128, 512], F32))
    t32_s = A(nc.sbuf_tensor("t32_s", [128, 512], F32))
    ab2_bf = A(nc.sbuf_tensor("ab2_bf", [128, 512], BF16))
    mb2_bf = A(nc.sbuf_tensor("mb2_bf", [128, 512], BF16))
    alpha_sb = A(nc.sbuf_tensor("alpha_sb", [128, 256], F32))
    z0_sb = A(nc.sbuf_tensor("z0_sb", [128, 256], F32))
    z1_sb = A(nc.sbuf_tensor("z1_sb", [128, 256], F32))
    ab_bf = A(nc.sbuf_tensor("ab_bf", [128, 512], BF16))
    mb_bf = A(nc.sbuf_tensor("mb_bf", [128, 512], BF16))
    kbar = A(nc.sbuf_tensor("kbar", [128, 48], F32))
    rt_scr = A(nc.sbuf_tensor("rt_scr", [128, 512], F32))
    prevc = A(nc.sbuf_tensor("prevc", [128, 8], BF16))
    eps_sb = A(nc.sbuf_tensor("eps_sb", [128, 1], F32))
    invT_sb = A(nc.sbuf_tensor("invT_sb", [128, 1], F32))
    ones_sb = A(nc.sbuf_tensor("ones_sb", [128, 128], BF16))
    ident_sb = A(nc.sbuf_tensor("ident_sb", [128, 128], F32))
    indic_sb = A(nc.sbuf_tensor("indic_sb", [16, 1024], BF16))
    bqkv_sb = A(nc.sbuf_tensor("bqkv_sb", [128, 24], F32))
    bo_sb = A(nc.sbuf_tensor("bo_sb", [128, 8], F32))
    mur_sb = A(nc.sbuf_tensor("mur_sb", [128, 8], F32))
    muk_sb = A(nc.sbuf_tensor("muk_sb", [128, 8], F32))

    pacc01 = A(nc.psum_tensor("pacc01", [128, 1024], F32))
    sg0 = A(nc.psum_tensor("sg0", [128, 1024], F32))
    sg1 = A(nc.psum_tensor("sg1", [128, 1024], F32))
    sg2 = A(nc.psum_tensor("sg2", [128, 1024], F32))

    def PA(i):
        return pacc01[:, (i % 2) * 512:((i % 2) + 1) * 512]
    sgs = [sg0, sg1, sg2]

    sem_names = ["g_x", "g_x2", "g_w0", "g_w1", "g_w2", "g_w3", "s_ld", "s_xr0",
                 "s_xr1", "s_st0", "s_st1", "t_mm", "v_dn", "sc_dn"]
    sems = {n: A(nc.semaphore(n)) for n in sem_names}
    block = A(nc.Block())

    qsched = []
    for s in range(NSEQ):
        for qi in range(4):
            qsched.append(QW["wq"] + qi)
            qsched.append(QW["wka"] + qi)
        for b in range(NBLK):
            for m in ("wvi", "wo", "wfk"):
                for qi in range(4):
                    qsched.append(QW[m] + qi)
            for qi in range(4):
                qsched.append(QW["wr"] + qi)
                qsched.append(QW["wfv"] + qi)
    NQS = len(qsched)

    def walk(en, engs):
        w = Walk(nc, en, engs, sems)
        slot_loads = [0, 0, 0]
        slot_free_t = [0, 0, 0]
        qpos = 0

        def load_quarter(i):
            nonlocal qpos
            assert i == qpos
            qpos += 1
            slot = i % 3
            src_q = qsched[i]
            w.wait("gpsimd", "t_mm", slot_free_t[slot])
            n = w.op("gpsimd",
                     lambda g, slot=slot, src_q=src_q: g.dma_start(
                         wbuf[:, slot * 2048:(slot + 1) * 2048], wpk[src_q, :, :]),
                     inc=f"g_w{slot}", inc_amt=16)
            slot_loads[slot] = n

        def use_quarter(i, last_mm_cnt):
            slot = i % 3
            slot_free_t[slot] = last_mm_cnt
            if qpos == i + 3 and qpos < NQS:
                load_quarter(i + 3)

        def wslot(i):
            return i % 3

        def wtile(i, dcl, cc):
            sl = wslot(i)
            return wbuf[:, sl * 2048 + dcl * 1024 + cc * 128:
                        sl * 2048 + dcl * 1024 + cc * 128 + 128]

        for fn in (lambda e: e.dma_start(ones_sb[:], ones_in[:, :]),
                   lambda e: e.dma_start(ident_sb[:], ident_in[:, :]),
                   lambda e: e.dma_start(indic_sb[:], indic_in[:, :]),
                   lambda e: e.dma_start(bqkv_sb[:], bqkv[:, :]),
                   lambda e: e.dma_start(bo_sb[:], bo_in[:, :]),
                   lambda e: e.dma_start(mur_sb[:], mur_in[:, :]),
                   lambda e: e.dma_start(muk_sb[:], muk_in[:, :])):
            w.op("sync", fn, inc="s_ld", inc_amt=16)
        n_const = w.cnt["s_ld"]

        for i in range(3):
            load_quarter(i)

        w.op("vector", lambda e: e.memset(eps_sb[:, :], float(EPS)), inc="v_dn")
        w.op("vector", lambda e: e.memset(invT_sb[:, :], 1.0 / T), inc="v_dn")

        pacc_evt = [[("v_dn", 0)], [("v_dn", 0)]]

        def wait_pacc(engine, i):
            for sem_, val_ in pacc_evt[i]:
                w.wait(engine, sem_, val_)

        def set_pacc(i, sem_, val_):
            pacc_evt[i] = [(sem_, val_)]
        x2slots_free = [0, 0]
        sg_free_sc = [0, 0, 0]
        sg_free_v = [0, 0, 0]
        zfree_v = [0, 0, 0, 0]
        sg_rr = 0
        acol_free_v = [0, 0, 0, 0]
        acol_free_t = [0, 0, 0, 0]
        rt_free_v = 0
        rpsum_free_sc = 0
        last_ar1_reader_v = 0
        last_ar1_reader_t = 0
        ln_free_v = 0
        ln_free_t = 0
        att_last_t = 0
        xr_slot_free = [[("v_dn", 0)], [("v_dn", 0)]]
        out_slot_free = [0, 0]
        xr_loads = [0, 0]
        x_cast_cnt = 0

        carry = None

        def xr_issue(s, b, do, slot):
            for sem_, val_ in xr_slot_free[slot]:
                w.wait("sync", sem_, val_)
            n = w.op("sync",
                     lambda e, s=s, b=b, do=do, slot=slot: e.dma_start(
                         xr_stage[:, slot * 512:(slot + 1) * 512],
                         x_cm[s, do * 128:(do + 1) * 128, b * 512:(b + 1) * 512]),
                     inc=f"s_xr{slot}", inc_amt=16)
            xr_loads[slot] = n

        ln_fill_v = [0] * 4

        def gen_s2a(si):
            w.wait("gpsimd", "t_mm", att_last_t)
            cast0 = w.op(
                "gpsimd",
                lambda g, si=si: g.dma_start(
                    bass.AP(qb, 0, [[16384, 128], [2048, 4], [1, 2048]]),
                    bass.AP(x_cm, si * C * T, [[2048, 128], [128 * 2048, 4], [1, 2048]])),
                inc="g_x", inc_amt=16)
            cast1 = w.op(
                "gpsimd",
                lambda g, si=si: g.dma_start(
                    bass.AP(qb, 4 * 2048, [[16384, 128], [2048, 4], [1, 2048]]),
                    bass.AP(x_cm, si * C * T + 4 * 128 * 2048,
                            [[2048, 128], [128 * 2048, 4], [1, 2048]])),
                inc="g_x2", inc_amt=16)
            yield 2

            def emit_ln(tt):
                w.wait("vector", "t_mm", x2slots_free[0])
                w.wait("vector", "t_mm", x2slots_free[1])
                lnv = None
                w.wait("vector", "t_mm", ln_free_t)
                for cc in range(NCH):
                    sl0 = (cc % 2) * 512
                    w.op("vector",
                         lambda e, cc=cc, tt=tt, sl0=sl0: e.tensor_tensor(
                             e_scr[:, sl0:sl0 + 512],
                             qb[:, cc * 2048 + tt * 512: cc * 2048 + (tt + 1) * 512],
                             ab2_bf[:, :], op=OP.mult),
                         inc="v_dn", lag=2)
                    lnv = w.op("vector",
                               lambda e, cc=cc, tt=tt, sl0=sl0: e.tensor_tensor(
                                   ln[:, cc * 2048 + tt * 512: cc * 2048 + (tt + 1) * 512],
                                   e_scr[:, sl0:sl0 + 512],
                                   mb2_bf[:, :], op=OP.subtract),
                               inc="v_dn")
                ln_fill_v[tt] = lnv
                x2slots_free[0] = 0

            for tt in range(4):
                sg = tt % 3
                w.wait("tensor", "s_ld", n_const)
                w.wait("tensor", "sc_dn", sg_free_sc[sg])
                w.wait("tensor", "v_dn", sg_free_v[sg])
                mm_s1 = None
                for cc in range(NCH):
                    w.wait("tensor", "g_x" if cc < 4 else "g_x2",
                           cast0 if cc < 4 else cast1)
                    mm_s1 = w.op("tensor",
                                 lambda e, cc=cc, tt=tt, sg=sg: e.matmul(
                                     sgs[sg][:, 0:512],
                                     ones_sb[:, :],
                                     qb[:, cc * 2048 + tt * 512: cc * 2048 + (tt + 1) * 512],
                                     start=(cc == 0), stop=(cc == NCH - 1)),
                                 inc="t_mm" if cc == NCH - 1 else None)
                mm_s2 = None
                for cc in range(NCH):
                    xsl = cc % 2
                    w.wait("vector", "g_x" if cc < 4 else "g_x2",
                           cast0 if cc < 4 else cast1)
                    w.wait("vector", "sc_dn", max(sg_free_sc))
                    w.wait("vector", "t_mm", x2slots_free[xsl])
                    x2v = w.op("vector",
                               lambda e, cc=cc, tt=tt, xsl=xsl: e.tensor_tensor(
                                   e_scr[:, xsl * 512:(xsl + 1) * 512],
                                   qb[:, cc * 2048 + tt * 512: cc * 2048 + (tt + 1) * 512],
                                   qb[:, cc * 2048 + tt * 512: cc * 2048 + (tt + 1) * 512],
                                   op=OP.mult),
                               inc="v_dn", lag=1)
                    w.wait("tensor", "v_dn", x2v)
                    mm_s2 = w.op("tensor",
                                 lambda e, cc=cc, xsl=xsl, sg=sg: e.matmul(
                                     sgs[sg][:, 512:1024],
                                     ones_sb[:, :],
                                     e_scr[:, xsl * 512:(xsl + 1) * 512],
                                     start=(cc == 0), stop=(cc == NCH - 1)),
                                 inc="t_mm")
                    x2slots_free[xsl] = mm_s2
                yield 5
                w.wait("vector", "t_mm", mm_s1)
                pmv = w.op("vector", lambda e, sg=sg: e.tensor_copy(
                    xr_stage[:, 0:512], sgs[sg][:, 0:512]), inc="v_dn", lag=1)
                w.wait("vector", "t_mm", mm_s2)
                pav = w.op("vector", lambda e, sg=sg: e.tensor_copy(
                    xr_stage[:, 512:1024], sgs[sg][:, 512:1024]), inc="v_dn", lag=1)
                sg_free_v[sg] = pav
                w.op("vector", lambda e: e.tensor_tensor(
                    t32_s[:, :], xr_stage[:, 0:512], xr_stage[:, 0:512],
                    op=OP.mult), inc="v_dn", lag=1)
                vv = w.op("vector", lambda e: e.tensor_tensor(
                    xr_stage[:, 512:1024], xr_stage[:, 512:1024], t32_s[:, :],
                    op=OP.subtract), inc="v_dn")
                w.wait("scalar", "v_dn", vv)
                w.op("scalar", lambda e: e.activation(
                    xr_stage[:, 512:1024], xr_stage[:, 512:1024],
                    AF.Ln, bias=eps_sb[:, :]), inc="sc_dn")
                sc_a = w.op("scalar", lambda e: e.activation(
                    t32_s[:, :], xr_stage[:, 512:1024], AF.Exp, scale=-0.5), inc="sc_dn")
                if tt > 0:
                    emit_ln(tt - 1)
                w.wait("vector", "sc_dn", sc_a)
                w.op("vector", lambda e: e.tensor_tensor(
                    xr_stage[:, 0:512], xr_stage[:, 0:512], t32_s[:, :],
                    op=OP.mult), inc="v_dn")
                w.op("vector", lambda e: e.tensor_copy(ab2_bf[:, :], t32_s[:, :]), inc="v_dn", lag=1)
                mb2c = w.op("vector", lambda e: e.tensor_copy(
                    mb2_bf[:, :], xr_stage[:, 0:512]), inc="v_dn")
                yield 5
            emit_ln(3)
            yield 3
            xr_slot_free[0] = [("v_dn", mb2c)]
            xr_slot_free[1] = [("sc_dn", sc_a), ("v_dn", vv)]

        for s in range(NSEQ):
            sq = s * (8 + NBLK * 20)

            g2 = gen_s2a(s)
            if carry is not None:
                a3 = 0
                b3 = 0
                for c2 in g2:
                    a3 += c2
                    while carry is not None and b3 * 10 < a3 * 22:
                        try:
                            v = next(carry)
                        except StopIteration:
                            carry = None
                            break
                        if v >= 0:
                            b3 += v
                while carry is not None:
                    try:
                        next(carry)
                    except StopIteration:
                        carry = None
            else:
                for _ in g2:
                    pass

            kc_done_ch = [0] * NCH


            def emit_qkv_quarter(qi, which):
                gq = sq + qi * 2 + which
                slot = wslot(gq)
                is_k = which == 1
                dbase = qi * 2
                kevt = {}
                last_mm = 0
                for dcl in range(2):
                    d = dbase + dcl
                    for j in range(4):
                        tt = j
                        half = j % 2
                        w.wait("tensor", f"g_w{slot}", slot_loads[slot])
                        wait_pacc("tensor", half)
                        w.wait("tensor", "sc_dn", rpsum_free_sc)
                        w.wait("tensor", "v_dn", ln_fill_v[tt])
                        mmn = None
                        for cc in range(NCH):
                            mmn = w.op("tensor",
                                       lambda e, gq=gq, dcl=dcl, cc=cc, tt=tt, half=half: e.matmul(
                                           PA(half),
                                           wtile(gq, dcl, cc),
                                           ln[:, cc * 2048 + tt * 512: cc * 2048 + (tt + 1) * 512],
                                           start=(cc == 0), stop=(cc == NCH - 1)),
                                       inc="t_mm" if cc == NCH - 1 else None)
                        last_mm = mmn
                        use_sc = j % 2 == 0
                        dst = kb if is_k else qb
                        bcol = 8 + d if is_k else d
                        dsl = d * 2048 + j * 512
                        kslot = 4 * d + j
                        if use_sc:
                            w.wait("scalar", "t_mm", mmn)
                            if is_k:
                                ev = w.op("scalar",
                                          lambda e, dst=dst, dsl=dsl, bcol=bcol, kslot=kslot, half=half: e.activation(
                                              dst[:, dsl:dsl + 512],
                                              PA(half), AF.Identity,
                                              bias=bqkv_sb[:, bcol:bcol + 1],
                                              accum_out=kbar[:, kslot:kslot + 1]),
                                          inc="sc_dn", lag=1)
                            else:
                                ev = w.op("scalar",
                                          lambda e, dst=dst, dsl=dsl, bcol=bcol, half=half: e.activation(
                                              dst[:, dsl:dsl + 512],
                                              PA(half), AF.Identity,
                                              bias=bqkv_sb[:, bcol:bcol + 1]),
                                          inc="sc_dn", lag=1)
                            set_pacc(half, "sc_dn", ev)
                            kevt[j] = ("sc_dn", ev)
                        else:
                            w.wait("vector", "t_mm", mmn)
                            if is_k:
                                ev = w.op("vector",
                                          lambda e, dst=dst, dsl=dsl, bcol=bcol, kslot=kslot, half=half: e.tensor_scalar(
                                              dst[:, dsl:dsl + 512],
                                              PA(half), bqkv_sb[:, bcol:bcol + 1], 0.0,
                                              op0=OP.add, op1=OP.add,
                                              accum_out=kbar[:, kslot:kslot + 1]),
                                          inc="v_dn", lag=1)
                            else:
                                ev = w.op("vector",
                                          lambda e, dst=dst, dsl=dsl, bcol=bcol, half=half: e.tensor_scalar(
                                              dst[:, dsl:dsl + 512],
                                              PA(half), bqkv_sb[:, bcol:bcol + 1], 0.0,
                                              op0=OP.add, op1=OP.add),
                                          inc="v_dn", lag=1)
                            set_pacc(half, "v_dn", ev)
                            kevt[j] = ("v_dn", ev)
                    if is_k:
                        for j in range(4):
                            w.wait("vector", kevt[j][0], kevt[j][1])
                        w.op("vector", lambda e, d=d: e.tensor_tensor(
                            kbar[:, 32 + d:33 + d], kbar[:, 4 * d:4 * d + 1],
                            kbar[:, 4 * d + 1:4 * d + 2], op=OP.add), inc="v_dn")
                        w.op("vector", lambda e, d=d: e.tensor_tensor(
                            mean_s[:, 0:1], kbar[:, 4 * d + 2:4 * d + 3],
                            kbar[:, 4 * d + 3:4 * d + 4], op=OP.add), inc="v_dn")
                        w.op("vector", lambda e, d=d: e.tensor_tensor(
                            kbar[:, 32 + d:33 + d], kbar[:, 32 + d:33 + d],
                            mean_s[:, 0:1], op=OP.add), inc="v_dn")
                        w.op("vector", lambda e, d=d: e.tensor_scalar(
                            kbar[:, 32 + d:33 + d], kbar[:, 32 + d:33 + d],
                            invT_sb[:, 0:1], None, op0=OP.mult), inc="v_dn")
                        kc_done_ch[d] = w.op("vector",
                                             lambda e, d=d: e.tensor_scalar(
                                                 kb[:, d * 2048:(d + 1) * 2048],
                                                 kb[:, d * 2048:(d + 1) * 2048],
                                                 kbar[:, 32 + d:33 + d], None,
                                                 op0=OP.subtract),
                                             inc="v_dn")
                use_quarter(gq, last_mm)

            def emit_tile(p, lc, mh):
                nonlocal sg_rr
                bb = lc // 4
                gA = sg_rr % 3
                gB = (sg_rr + 1) % 3
                sg_rr += 2
                w.wait("tensor", "sc_dn", sg_free_sc[gA])
                w.wait("tensor", "sc_dn", sg_free_sc[gB])
                w.wait("tensor", "v_dn", sg_free_v[gA])
                w.wait("tensor", "v_dn", sg_free_v[gB])
                w.wait("tensor", "v_dn", kc_done_ch[p])
                mm_fill = [0, 0]
                for n in range(2):
                    for ab in range(2):
                        r0 = ab * 64
                        g = gA if ab == 0 else gB
                        mm_fill[ab] = w.op(
                            "tensor",
                            lambda e, p=p, lc=lc, mh=mh, n=n, r0=r0, g=g: e.matmul(
                                sgs[g][:, n * 512:(n + 1) * 512],
                                qb[r0:r0 + 64,
                                   p * 2048 + lc * 128: p * 2048 + (lc + 1) * 128],
                                kb[r0:r0 + 64,
                                   p * 2048 + mh * 1024 + n * 512:
                                   p * 2048 + mh * 1024 + (n + 1) * 512],
                                start=True, stop=True,
                                tile_position=(r0, 0)),
                            inc="t_mm" if n == 1 else None)
                for ab in range(2):
                    g = gA if ab == 0 else gB
                    zt = z0_sb if mh == 0 else z1_sb
                    col = lc * 16 + 2 * p + ab
                    w.wait("scalar", "t_mm", mm_fill[ab])
                    w.wait("scalar", "v_dn", zfree_v[bb])
                    sg_free_sc[g] = w.op(
                        "scalar",
                        lambda e, ab=ab, col=col, g=g, zt=zt: e.activation(
                            e_scr[:, ab * 1024:(ab + 1) * 1024],
                            sgs[g][:, :], AF.Exp,
                            scale=float(SCALE),
                            accum_out=zt[:, col:col + 1]),
                        inc="sc_dn", lag=1)

            for qi in range(4):
                emit_qkv_quarter(qi, 0)
                emit_qkv_quarter(qi, 1)
                for p in (2 * qi, 2 * qi + 1):
                    for lci in range(4):
                        for mh in range(2):
                            emit_tile(p, 0 * 4 + lci, mh)

            kc_done = max(kc_done_ch)

            def gen_attn(b, kc_done):
                for lci in range(4):
                    lc = b * 4 + lci
                    for p in range(NCH):
                        for mh in range(2):
                            emit_tile(p, lc, mh)
                            yield 4

            def gen_post(b, qbase, si):
                nonlocal last_ar1_reader_v, last_ar1_reader_t, rt_free_v, rpsum_free_sc
                w.wait("vector", "sc_dn", max(sg_free_sc))
                w.wait("vector", "v_dn", acol_free_v[b])
                w.wait("vector", "t_mm", acol_free_t[b])
                zfree_v[b] = w.op("vector", lambda e, b=b: e.tensor_tensor(
                    alpha_sb[:, b * 64:(b + 1) * 64],
                    z0_sb[:, b * 64:(b + 1) * 64],
                    z1_sb[:, b * 64:(b + 1) * 64], op=OP.add), inc="v_dn")
                w.op("vector", lambda e, b=b: e.reciprocal(
                    var_s[:, 0:64], alpha_sb[:, b * 64:(b + 1) * 64]), inc="v_dn")
                amul = w.op("vector", lambda e, b=b: e.tensor_tensor(
                    alpha_sb[:, b * 64:(b + 1) * 64],
                    alpha_sb[:, b * 64:(b + 1) * 64],
                    var_s[:, 0:64], op=OP.mult), inc="v_dn")
                acol_free_v[b] = amul
                yield 6
                w.wait("tensor", "v_dn", amul)
                wait_pacc("tensor", 0)
                tcnt = 0
                for lci in range(4):
                    lc = b * 4 + lci
                    tcnt = w.op("tensor",
                                lambda e, lc=lc, lci=lci: e.transpose(
                                    pacc01[0:16, lci * 128:(lci + 1) * 128],
                                    alpha_sb[:, lc * 16:(lc + 1) * 16],
                                    ident_sb[:, :]),
                                inc="t_mm" if lci == 3 else None)
                acol_free_t[b] = tcnt
                w.wait("vector", "t_mm", tcnt)
                atv = w.op("vector",
                           lambda e: e.tensor_copy(
                               alphaT[0:16, :], pacc01[0:16, 0:512]),
                           inc="v_dn", lag=1)
                set_pacc(0, "v_dn", atv)
                alphaT_ready_v = atv
                yield 6

                for do in range(2):
                    xr_issue(si, b, do, do % 2)
                vop = 0
                for dv in range(NCH):
                    gq = qbase + (dv // 2)
                    slot = wslot(gq)
                    dcl = dv % 2
                    pa = dv % 2
                    pb = 1 - pa
                    w.wait("tensor", f"g_w{slot}", slot_loads[slot])
                    wait_pacc("tensor", pa)
                    mmv = None
                    for cc in range(NCH):
                        mmv = w.op("tensor",
                                   lambda e, gq=gq, dcl=dcl, cc=cc, b=b, pa=pa: e.matmul(
                                       PA(pa),
                                       wtile(gq, dcl, cc),
                                       ln[:, cc * 2048 + b * 512: cc * 2048 + (b + 1) * 512],
                                       start=(cc == 0), stop=(cc == NCH - 1)),
                                   inc="t_mm" if cc == NCH - 1 else None)
                    if dv % 2 == 1:
                        use_quarter(gq, mmv)
                    w.wait("tensor", "v_dn", alphaT_ready_v)
                    wait_pacc("tensor", pb)
                    abmm = w.op("tensor",
                                lambda e, dv=dv, pb=pb: e.matmul(
                                    PA(pb),
                                    indic_sb[:, dv * 128:(dv + 1) * 128],
                                    alphaT[0:16, :],
                                    start=True, stop=True),
                                inc="t_mm")
                    vsl = (dv % 2) * 512
                    w.wait("vector", "t_mm", mmv)
                    vev = w.op("vector",
                               lambda e, dv=dv, pa=pa, vsl=vsl: e.tensor_scalar(
                                   v_scr[:, vsl:vsl + 512], PA(pa),
                                   bqkv_sb[:, 16 + dv:17 + dv], 0.0,
                                   op0=OP.add, op1=OP.add),
                               inc="v_dn", lag=1)
                    set_pacc(pa, "v_dn", vev)
                    w.wait("vector", "t_mm", abmm)
                    vop = w.op("vector",
                               lambda e, dv=dv, vsl=vsl, pb=pb: e.tensor_tensor(
                                   opt[:, dv * 512:(dv + 1) * 512],
                                   v_scr[:, vsl:vsl + 512], PA(pb), op=OP.mult),
                               inc="v_dn")
                    set_pacc(pb, "v_dn", vop)
                    yield 9
                opt_fill_v = vop
                yield -1000 - b

                x1v = 0
                for do in range(NCH):
                    gq = qbase + 4 + (do // 2)
                    slot = wslot(gq)
                    dcl = do % 2
                    pa = do % 2
                    w.wait("tensor", f"g_w{slot}", slot_loads[slot])
                    wait_pacc("tensor", pa)
                    w.wait("tensor", "v_dn", opt_fill_v)
                    mmo = None
                    for dv in range(NCH):
                        mmo = w.op("tensor",
                                   lambda e, gq=gq, dcl=dcl, dv=dv, pa=pa: e.matmul(
                                       PA(pa),
                                       wtile(gq, dcl, dv),
                                       opt[:, dv * 512:(dv + 1) * 512],
                                       start=(dv == 0), stop=(dv == NCH - 1)),
                                   inc="t_mm" if dv == NCH - 1 else None)
                    if do % 2 == 1:
                        use_quarter(gq, mmo)
                    xsl = do % 2
                    w.wait("vector", "t_mm", mmo)
                    w.wait("vector", f"s_xr{xsl}", xr_loads[xsl])
                    x1v = w.op("vector",
                               lambda e, do=do, pa=pa, xsl=xsl: e.scalar_tensor_tensor(
                                   x1[:, do * 512:(do + 1) * 512],
                                   PA(pa), bo_sb[:, do:do + 1],
                                   xr_stage[:, xsl * 512:(xsl + 1) * 512],
                                   op0=OP.add, op1=OP.add),
                               inc="v_dn", lag=2)
                    set_pacc(pa, "v_dn", x1v)
                    xr_slot_free[xsl] = [("v_dn", x1v)]
                    if do + 2 < NCH:
                        xr_issue(si, b, do + 2, (do + 2) % 2)
                    yield 8

                yield -1 - b
                xbv = w.op("vector", lambda e: e.tensor_copy(
                    ar1[:, 0:4096], x1[:, :]), inc="v_dn")
                last_ar1_reader_v = xbv
                w.wait("tensor", "v_dn", xbv)
                wait_pacc("tensor", 0)
                mm_s1 = None
                for cc in range(NCH):
                    mm_s1 = w.op("tensor",
                                 lambda e, cc=cc: e.matmul(
                                     PA(0), ones_sb[:, :],
                                     ar1[:, cc * 512:(cc + 1) * 512],
                                     start=(cc == 0), stop=(cc == NCH - 1)),
                                 inc="t_mm" if cc == NCH - 1 else None)
                mm_s2 = None
                for cc in range(NCH):
                    xsl = cc % 2
                    w.wait("vector", "t_mm", x2slots_free[xsl])
                    x2v = w.op("vector",
                               lambda e, cc=cc, xsl=xsl: e.tensor_tensor(
                                   ar1[:, 8192 + xsl * 512: 8192 + (xsl + 1) * 512],
                                   ar1[:, cc * 512:(cc + 1) * 512],
                                   ar1[:, cc * 512:(cc + 1) * 512],
                                   op=OP.mult),
                               inc="v_dn", lag=(0 if cc == 0 else 1))
                    w.wait("tensor", "v_dn", x2v)
                    if cc == 0:
                        wait_pacc("tensor", 1)
                    mm_s2 = w.op("tensor",
                                 lambda e, cc=cc, xsl=xsl: e.matmul(
                                     PA(1), ones_sb[:, :],
                                     ar1[:, 8192 + xsl * 512: 8192 + (xsl + 1) * 512],
                                     start=(cc == 0), stop=(cc == NCH - 1)),
                                 inc="t_mm")
                    x2slots_free[xsl] = mm_s2
                yield 16
                w.wait("vector", "t_mm", mm_s1)
                set_pacc(0, "v_dn", w.op("vector", lambda e: e.tensor_copy(
                    mean_s[:, :], PA(0)), inc="v_dn", lag=1))
                w.wait("vector", "t_mm", mm_s2)
                set_pacc(1, "v_dn", w.op("vector", lambda e: e.tensor_copy(
                    var_s[:, :], PA(1)), inc="v_dn", lag=1))
                w.op("vector", lambda e: e.tensor_tensor(
                    t3_s[:, :], mean_s[:, :], mean_s[:, :], op=OP.mult), inc="v_dn", lag=1)
                vv = w.op("vector", lambda e: e.tensor_tensor(
                    var_s[:, :], var_s[:, :], t3_s[:, :], op=OP.subtract), inc="v_dn")
                w.wait("scalar", "v_dn", vv)
                w.op("scalar", lambda e: e.activation(
                    var_s[:, :], var_s[:, :], AF.Ln, bias=eps_sb[:, :]), inc="sc_dn")
                sc_a = w.op("scalar", lambda e: e.activation(
                    t3_s[:, :], var_s[:, :], AF.Exp, scale=-0.5), inc="sc_dn")
                w.wait("vector", "sc_dn", sc_a)
                w.op("vector", lambda e: e.tensor_tensor(
                    mean_s[:, :], mean_s[:, :], t3_s[:, :], op=OP.mult), inc="v_dn")
                w.op("vector", lambda e: e.tensor_copy(ab_bf[:, :], t3_s[:, :]), inc="v_dn", lag=1)
                w.op("vector", lambda e: e.tensor_copy(mb_bf[:, :], mean_s[:, :]), inc="v_dn")
                yield 4
                w.wait("vector", "t_mm", max(x2slots_free))
                for cc in range(NCH):
                    w.op("vector",
                         lambda e, cc=cc: e.tensor_tensor(
                             ar1[:, 8192 + cc * 512: 8192 + (cc + 1) * 512],
                             ar1[:, cc * 512:(cc + 1) * 512],
                             ab_bf[:, :], op=OP.mult),
                         inc="v_dn", lag=2)
                    w.op("vector",
                         lambda e, cc=cc: e.tensor_tensor(
                             ar1[:, 4096 + cc * 512: 4096 + (cc + 1) * 512],
                             ar1[:, 8192 + cc * 512: 8192 + (cc + 1) * 512],
                             mb_bf[:, :], op=OP.subtract),
                         inc="v_dn")
                    if cc % 2 == 1:
                        yield 3
                if b == 0:
                    w.op("vector", lambda e: e.memset(prevc[:, :], 0.0), inc="v_dn")
                yield 4
                for cc in range(NCH):
                    c0 = 4096 + cc * 512
                    dd0 = 8192 + cc * 512
                    w.op("vector",
                         lambda e, c0=c0, dd0=dd0: e.tensor_tensor(
                             bass.AP(ar1, dd0 + 1, [[16384, 128], [1, 511]]),
                             bass.AP(ar1, c0 + 1, [[16384, 128], [1, 511]]),
                             bass.AP(ar1, c0, [[16384, 128], [1, 511]]),
                             op=OP.subtract),
                         inc="v_dn", lag=2)
                w.op("vector",
                     lambda e: e.tensor_tensor(
                         bass.AP(ar1, 8192, [[16384, 128], [512, 8]]),
                         bass.AP(ar1, 4096, [[16384, 128], [512, 8]]),
                         prevc[:, 0:8], op=OP.subtract),
                     inc="v_dn", lag=1)
                yield 6
                kready = [0] * NCH
                for cc in range(NCH):
                    c0 = 4096 + cc * 512
                    dd0 = 8192 + cc * 512
                    w.op("vector",
                         lambda e, c0=c0, dd0=dd0, cc=cc: e.scalar_tensor_tensor(
                             bass.AP(ar1, 12288 + cc * 512 + 1, [[16384, 128], [1, 511]]),
                             bass.AP(ar1, dd0 + 1, [[16384, 128], [1, 511]]),
                             mur_sb[:, cc:cc + 1],
                             bass.AP(ar1, c0, [[16384, 128], [1, 511]]),
                             op0=OP.mult, op1=OP.add),
                         inc="v_dn", lag=2)
                    w.op("vector",
                         lambda e, dd0=dd0, cc=cc: e.scalar_tensor_tensor(
                             ar1[:, 12288 + cc * 512: 12288 + cc * 512 + 1],
                             ar1[:, dd0:dd0 + 1], mur_sb[:, cc:cc + 1],
                             prevc[:, cc:cc + 1], op0=OP.mult, op1=OP.add),
                         inc="v_dn")
                    w.op("vector",
                         lambda e, c0=c0, dd0=dd0, cc=cc: e.scalar_tensor_tensor(
                             bass.AP(ar6, cc * 512 + 1, [[4096, 128], [1, 511]]),
                             bass.AP(ar1, dd0 + 1, [[16384, 128], [1, 511]]),
                             muk_sb[:, cc:cc + 1],
                             bass.AP(ar1, c0, [[16384, 128], [1, 511]]),
                             op0=OP.mult, op1=OP.add),
                         inc="v_dn", lag=2)
                    kready[cc] = w.op("vector",
                                      lambda e, dd0=dd0, cc=cc: e.scalar_tensor_tensor(
                                          ar6[:, cc * 512: cc * 512 + 1],
                                          ar1[:, dd0:dd0 + 1], muk_sb[:, cc:cc + 1],
                                          prevc[:, cc:cc + 1], op0=OP.mult, op1=OP.add),
                                      inc="v_dn")
                    if cc % 2 == 1:
                        yield 4
                mixv = w.op("vector", lambda e: e.tensor_copy(
                    prevc[:, :],
                    bass.AP(ar1, 4096 + 511, [[16384, 128], [512, 8]])), inc="v_dn")
                last_ar1_reader_v = mixv
                yield 8

                kav = 0
                for dd in range(NCH):
                    gq = qbase + 8 + (dd // 2)
                    slot = wslot(gq)
                    dcl = dd % 2
                    pa = dd % 2
                    w.wait("tensor", f"g_w{slot}", slot_loads[slot])
                    wait_pacc("tensor", pa)
                    mmk = None
                    for cc in range(NCH):
                        w.wait("tensor", "v_dn", kready[cc])
                        mmk = w.op("tensor",
                                   lambda e, gq=gq, dcl=dcl, cc=cc, pa=pa: e.matmul(
                                       PA(pa),
                                       wtile(gq, dcl, cc),
                                       ar6[:, cc * 512:(cc + 1) * 512],
                                       start=(cc == 0), stop=(cc == NCH - 1)),
                                   inc="t_mm" if cc == NCH - 1 else None)
                    if dd % 2 == 1:
                        use_quarter(gq, mmk)
                    w.wait("vector", "t_mm", mmk)
                    kasc = var_s if dd % 2 == 0 else mean_s
                    w.op("vector",
                         lambda e, pa=pa, kasc=kasc: e.tensor_scalar(
                             kasc[:, :], PA(pa), 0.0, None, op0=OP.max),
                         inc="v_dn", lag=2)
                    kav = w.op("vector",
                               lambda e, dd=dd, kasc=kasc: e.tensor_tensor(
                                   ar1[:, 8192 + dd * 512: 8192 + (dd + 1) * 512],
                                   kasc[:, :], kasc[:, :], op=OP.mult),
                               inc="v_dn")
                    set_pacc(pa, "v_dn", kav)
                    yield 8
                last_ar1_reader_v = kav

                mmf = 0
                for dd in range(NCH):
                    gqr = qbase + 12 + 2 * (dd // 2)
                    gqv = gqr + 1
                    dcl = dd % 2
                    sr = wslot(gqr)
                    sv = wslot(gqv)
                    pfr = dd % 2
                    pff = 1 - pfr
                    w.wait("tensor", f"g_w{sr}", slot_loads[sr])
                    w.wait("tensor", "v_dn", kav)
                    wait_pacc("tensor", pfr)
                    w.wait("tensor", "sc_dn", rpsum_free_sc)
                    mmr = None
                    for cc in range(NCH):
                        mmr = w.op("tensor",
                                   lambda e, gqr=gqr, dcl=dcl, cc=cc, pfr=pfr: e.matmul(
                                       PA(pfr),
                                       wtile(gqr, dcl, cc),
                                       ar1[:, 12288 + cc * 512: 12288 + (cc + 1) * 512],
                                       start=(cc == 0), stop=(cc == NCH - 1)),
                                   inc="t_mm" if cc == NCH - 1 else None)
                    yield 8
                    w.wait("tensor", f"g_w{sv}", slot_loads[sv])
                    wait_pacc("tensor", pff)
                    mmf = None
                    for cc in range(NCH):
                        mmf = w.op("tensor",
                                   lambda e, gqv=gqv, dcl=dcl, cc=cc, pff=pff: e.matmul(
                                       PA(pff),
                                       wtile(gqv, dcl, cc),
                                       ar1[:, 8192 + cc * 512: 8192 + (cc + 1) * 512],
                                       start=(cc == 0), stop=(cc == NCH - 1)),
                                   inc="t_mm" if cc == NCH - 1 else None)
                    if dd % 2 == 1:
                        use_quarter(gqr, mmf)
                        use_quarter(gqv, mmf)
                    w.wait("scalar", "t_mm", mmr)
                    w.wait("scalar", "v_dn", rt_free_v)
                    rpsum_free_sc = w.op("scalar",
                         lambda e, pfr=pfr: e.activation(
                             rt_scr[:, :], PA(pfr), AF.Exp, scale=-1.0),
                         inc="sc_dn", lag=1)
                    w.op("scalar", lambda e: e.activation(
                        rt_scr[:, :], rt_scr[:, :], AF.Ln, bias=1.0), inc="sc_dn")
                    scs = w.op("scalar", lambda e: e.activation(
                        rt_scr[:, :], rt_scr[:, :], AF.Exp, scale=-1.0), inc="sc_dn")
                    osl = dd % 2
                    w.wait("vector", "sc_dn", scs)
                    w.wait("vector", "t_mm", mmf)
                    w.wait("vector", f"s_st{osl}", out_slot_free[osl])
                    fsc = t3_s if dd % 2 == 0 else mean_s
                    t1v = w.op("vector",
                               lambda e, pff=pff, fsc=fsc: e.tensor_tensor(
                                   fsc[:, :], PA(pff), rt_scr[:, :], op=OP.mult),
                               inc="v_dn", lag=1)
                    rt_free_v = t1v
                    set_pacc(pff, "v_dn", t1v)
                    ov = w.op("vector",
                              lambda e, dd=dd, osl=osl, fsc=fsc: e.tensor_tensor(
                                  out_stage[:, osl * 512:(osl + 1) * 512],
                                  fsc[:, :], x1[:, dd * 512:(dd + 1) * 512],
                                  op=OP.add),
                              inc="v_dn")
                    w.wait("sync", "v_dn", ov)
                    stn = w.op("sync",
                               lambda e, si=si, b=b, dd=dd, osl=osl: e.dma_start(
                                   out_cm[si, dd * 128:(dd + 1) * 128,
                                          b * 512:(b + 1) * 512],
                                   out_stage[:, osl * 512:(osl + 1) * 512]),
                               inc=f"s_st{osl}", inc_amt=16)
                    out_slot_free[osl] = stn
                    yield 8
                last_ar1_reader_t = w.cnt["t_mm"]

            for b in range(NBLK):
                a_cost = 0
                b_cost = 0
                if b > 0:
                    ga = gen_attn(b, kc_done)
                    for c in ga:
                        a_cost += c
                        while carry is not None and b_cost * 20 < max(0, a_cost - 8) * 27:
                            try:
                                v = next(carry)
                                if v >= 0:
                                    b_cost += v
                            except StopIteration:
                                carry = None
                if b == NBLK - 1:
                    att_last_t = w.cnt["t_mm"]
                ng = gen_post(b, sq + 8 + b * 20, s)
                carry = ng if carry is None else chain(carry, ng)
            if s == NSEQ - 1:
                for _ in carry:
                    pass
                carry = None
            else:
                for c in carry:
                    if c == -1 - (NBLK - 1):
                        break
                ln_free_t = w.cnt["t_mm"]

        w.wait("sync", "s_st0", w.cnt["s_st0"])
        w.wait("sync", "s_st1", w.cnt["s_st1"])
        return w

    @block.tensor
    def _(e):
        walk("tensor", {"tensor": e})

    @block.vector
    def _(e):
        walk("vector", {"vector": e})

    @block.scalar
    def _(e):
        walk("scalar", {"scalar": e})

    @block.gpsimd
    def _(e):
        walk("gpsimd", {"gpsimd": e})

    @block.sync
    def _(e):
        walk("sync", {"sync": e})

    ctx.close()
    return nc


def _pack_quarters(MT):
    t = MT.reshape(8, 128, 8, 128)
    out = np.empty((4, 128, 2048), np.float32)
    for qi in range(4):
        blk = t[:, :, 2 * qi:2 * qi + 2, :]
        out[qi] = blk.transpose(1, 2, 0, 3).reshape(128, 2048)
    return out


def _host_prep(inputs):
    w_in = np.asarray(inputs["L0_attn_w_in"], np.float32)
    b_in = np.asarray(inputs["L0_attn_b_in"], np.float32)
    perm_q = [h * 192 + i for h in range(NH) for i in range(64)]
    perm_k = [h * 192 + 64 + i for h in range(NH) for i in range(64)]
    perm_v = [h * 192 + 128 + i for h in range(NH) for i in range(64)]
    wqT = np.ascontiguousarray(w_in[perm_q].T)
    wkT = np.ascontiguousarray(w_in[perm_k].T)
    wvT = np.ascontiguousarray(w_in[perm_v].T)
    woT = np.ascontiguousarray(np.asarray(inputs["L0_attn_w_out"], np.float32).T)
    wrT = np.ascontiguousarray(np.asarray(inputs["L0_ffn_W_r"], np.float32).T)
    wfkT = np.ascontiguousarray(np.asarray(inputs["L0_ffn_W_k"], np.float32).T)
    wfvT = np.ascontiguousarray(np.asarray(inputs["L0_ffn_W_v"], np.float32).T)

    wpk = np.empty((NQ, 128, 2048), np.float32)
    wpk[QW["wq"]:QW["wq"] + 4] = _pack_quarters(wqT)
    wpk[QW["wka"]:QW["wka"] + 4] = _pack_quarters(wkT)
    wpk[QW["wvi"]:QW["wvi"] + 4] = _pack_quarters(wvT)
    wpk[QW["wo"]:QW["wo"] + 4] = _pack_quarters(woT)
    wpk[QW["wr"]:QW["wr"] + 4] = _pack_quarters(wrT)
    wpk[QW["wfk"]:QW["wfk"] + 4] = _pack_quarters(wfkT)
    wpk[QW["wfv"]:QW["wfv"] + 4] = _pack_quarters(wfvT)

    bqkv = np.empty((128, 24), np.float32)
    bqkv[:, 0:8] = b_in[perm_q].reshape(8, 128).T
    bqkv[:, 8:16] = b_in[perm_k].reshape(8, 128).T
    bqkv[:, 16:24] = b_in[perm_v].reshape(8, 128).T
    bo = np.ascontiguousarray(
        np.asarray(inputs["L0_attn_b_out"], np.float32).reshape(8, 128).T)
    mur = np.ascontiguousarray(
        np.asarray(inputs["L0_ffn_mu_r"], np.float32).reshape(8, 128).T)
    muk = np.ascontiguousarray(
        np.asarray(inputs["L0_ffn_mu_k"], np.float32).reshape(8, 128).T)

    ones128 = np.full((128, 128), 1.0 / C, ml_dtypes.bfloat16)
    ident = np.eye(128, dtype=np.float32)
    indic = np.zeros((16, 1024), ml_dtypes.bfloat16)
    for dv in range(8):
        for p in range(128):
            indic[2 * dv + p // 64, dv * 128 + p] = 1.0
    return dict(wpk=wpk, bqkv=bqkv, bo_in=bo, mur_in=mur, muk_in=muk,
                ones_in=ones128, ident_in=ident, indic_in=indic)


def kernel(**inputs):
    x = np.asarray(inputs["L0_attn_x"], np.float32)
    shared = _host_prep(inputs)
    nc = build_program()
    in_maps = []
    for core in range(8):
        xs = x[core * NSEQ:(core + 1) * NSEQ]
        x_cmv = np.ascontiguousarray(xs.transpose(0, 2, 1))
        m = dict(shared)
        m["x_cm"] = x_cmv
        in_maps.append(m)
    res = run_bass_kernel_spmd(nc, in_maps, core_ids=list(range(8)))
    outs = [r["out_cm"] for r in res.results]
    out = np.concatenate(outs, axis=0).transpose(0, 2, 1)
    return np.ascontiguousarray(out).astype(np.float32)


if __name__ == "__main__":
    nc = build_program()
    print("built OK")
